# version 23
# baseline (speedup 1.0000x reference)
import numpy as np
import ml_dtypes
from contextlib import ExitStack

import concourse.bass as bass
import concourse.mybir as mybir
from concourse.bass_utils import run_bass_kernel_spmd

F32 = mybir.dt.float32
BF16 = mybir.dt.bfloat16
AF = mybir.ActivationFunctionType
ALU = mybir.AluOpType

T, D, H, DH, FC = 2048, 1024, 16, 64, 8
TT = 256
NT = T // TT
C = 128
CPT = TT // C
HID, HC, NE = 1408, 11, 8
ST = 512
NST = T // ST
EPS = 1e-5


class Sem:
    __slots__ = ("h", "c", "name")

    def __init__(self, h, name):
        self.h, self.c, self.name = h, 0, name

    def tok(self):
        return (self, self.c)


class Prog:
    def __init__(self, nc):
        self.nc = nc
        self.ops = []
        self.pools = {}

    def sem(self, name):
        return Sem(self.nc.alloc_semaphore(name), name)

    def make_pool(self, name, eng, n):
        self.pools[eng] = [self.sem(f"{name}{i}") for i in range(n)]
        self.pools[eng + "_idx"] = 0

    def dma(self, eng, fn, waits=()):
        pool = self.pools[eng]
        i = self.pools[eng + "_idx"]
        self.pools[eng + "_idx"] = (i + 1) % len(pool)
        sem = pool[i]
        w = list(waits) + ([sem.tok()] if sem.c > 0 else [])
        return self.op(eng, fn, waits=w, inc=sem, inc_n=16)

    def pool_toks(self, eng):
        return [sm.tok() for sm in self.pools[eng]]

    def op(self, eng, fn, waits=(), inc=None, inc_n=1):
        incs = []
        if inc is not None:
            inc.c += inc_n
            incs.append((inc, inc_n))
        self.ops.append((eng, fn,
                         [(s, v) for (s, v) in waits if s is not None and v > 0],
                         incs))
        return inc.tok() if inc is not None else None

    def emit(self):
        def mk(engname):
            def body(eng):
                waited = {}
                for (e, fn, waits, incs) in self.ops:
                    if e != engname:
                        continue
                    for (s, v) in waits:
                        if waited.get(id(s), 0) < v:
                            eng.wait_ge(s.h, v)
                            waited[id(s)] = v
                    ins = fn(eng)
                    for (s, n) in incs:
                        ins = ins.then_inc(s.h, n)
            return body

        with self.nc.Block() as block:
            block.sync(mk("sync"))
            block.tensor(mk("tensor"))
            block.vector(mk("vector"))
            block.scalar(mk("scalar"))
            block.gpsimd(mk("gpsimd"))


def build(dbg=(), stop_after="full"):
    nc = bass.Bass("TRN2", target_bir_lowering=False)
    p = Prog(nc)
    dbg = set(dbg)

    d_x = nc.dram_tensor("x_t", [D, T], F32, kind="ExternalInput").ap()
    d_W = {m: nc.dram_tensor(f"w{m}T", [D, D], F32, kind="ExternalInput").ap()
           for m in "rkvwo"}
    d_u = nc.dram_tensor("u_pp", [128, FC], F32, kind="ExternalInput").ap()
    d_tm = {m: nc.dram_tensor(f"tm{m}_pp", [128, FC], F32, kind="ExternalInput").ap()
            for m in "rkvw"}
    d_lnw = nc.dram_tensor("lnw_row", [D], F32, kind="ExternalInput").ap()
    d_lnb = nc.dram_tensor("lnb_row", [D], F32, kind="ExternalInput").ap()
    d_gw = nc.dram_tensor("gate_wT", [D, NE], F32, kind="ExternalInput").ap()
    d_eb = nc.dram_tensor("ebias", [NE], F32, kind="ExternalInput").ap()
    d_wup = nc.dram_tensor("w_upT", [NE, D, HID], F32, kind="ExternalInput").ap()
    d_wdn = nc.dram_tensor("w_downT", [NE, HID, D], F32, kind="ExternalInput").ap()
    d_sup = nc.dram_tensor("sh_upT", [D, HID], F32, kind="ExternalInput").ap()
    d_sdn = nc.dram_tensor("sh_downT", [HID, D], F32, kind="ExternalInput").ap()
    d_out = nc.dram_tensor("out_t", [D, T], F32, kind="ExternalOutput").ap()

    d_xnbf = nc.dram_tensor("xnbf_scr", [128, FC, T], BF16, kind="Internal").ap()
    d_s = nc.dram_tensor("s_scr", [128, FC, T], F32, kind="Internal").ap()
    d_sel = nc.dram_tensor("sel_scr", [NE, T], F32, kind="Internal").ap()

    d_dbg = {}
    dbg_shapes = {"xn": [128, FC, T], "scores": [NE, T], "sel": [NE, T],
                  "w": [128, FC, T], "k": [128, FC, T], "r": [128, FC, T],
                  "vtm": [T // C, 128, D], "otm": [T // C, 128, D],
                  "obar": [T // C, 128, D], "sattn": [128, FC, T],
                  "g": [128, FC, T], "zw": [128, FC, T],
                  "btm": [NT * CPT, 128, NE], "stm": [NT * CPT, 128, NE],
                  "sel8": [NT * CPT, 128, NE], "gt3": [NT * CPT, 128, NE * NE],
                  "eq3": [NT * CPT, 128, NE * NE], "rkA": [NT * CPT, 128, NE],
                  "rkB": [NT * CPT, 128, NE], "rkT": [NT * CPT, 128, NE]}
    for name in dbg:
        d_dbg[name] = nc.dram_tensor("dbg_" + name, dbg_shapes[name], F32,
                                     kind="ExternalOutput").ap()

    c_ident = nc.inline_tensor(np.eye(128, dtype=np.float32), "identc")
    _b2 = np.zeros((128, 2), dtype=ml_dtypes.bfloat16)
    _b2[0:64, 0] = 1
    _b2[64:128, 1] = 1
    c_ones2 = nc.inline_tensor(_b2, "ones2c")
    CSH = 0.8
    _t = np.arange(128, dtype=np.float64)
    _msk = np.triu(np.ones((128, 128)), 1) * np.exp(
        -CSH * (_t[None, :] - _t[:, None] - 1.0))
    c_mask = nc.inline_tensor(_msk.astype(np.float32), "maskc")
    c_rampinv = nc.inline_tensor(
        np.broadcast_to(np.exp(-CSH * (_t - 63.0)), (128, 128)).astype(np.float32),
        "rampinvc")
    c_mask4 = nc.inline_tensor(
        np.tile(_msk.astype(np.float32), (1, 4)), "mask4c")
    c_lcol = nc.inline_tensor(
        np.exp(CSH * (_t - 63.0)).astype(np.float32)[:, None], "lcolc")

    p.make_pool("dL", "sync", 6)
    p.make_pool("dS", "gpsimd", 4)
    tC = p.sem("tC")
    vC = p.sem("vC")
    sC = p.sem("sC")
    gC = p.sem("gC")

    A = lambda name, shape, dt=F32: nc.alloc_sbuf_tensor(name, shape, dt).ap()
    sb_ones = A("ones_f", [128, 128])
    sb_ones_bf = A("ones_bf", [128, 2], BF16)
    sb_ident = A("idents", [128, 128])
    sb_mask = A("masks", [128, 128])
    sb_lcol = A("lcol", [128, 1])
    sb_nc63 = A("nc63", [128, 1])
    sb_u = A("u_sb", [128, FC])
    sb_tm = {m: A(f"tm_{m}", [128, FC]) for m in "rkvw"}
    sb_gw = A("gw_sb", [128, FC * NE])
    sb_eb = A("eb_sb", [NE, 1])

    PA = nc.alloc_psum_tensor("PA", [128, 512], F32).ap()
    PB = nc.alloc_psum_tensor("PB", [128, 512], F32).ap()
    PCfull = nc.alloc_psum_tensor("PC", [128, 1024], F32).ap()
    PC0 = PCfull[:, 0:512]
    PC1 = PCfull[:, 512:1024]
    PD = nc.alloc_psum_tensor("PD", [128, 512], F32).ap()
    PEb = nc.alloc_psum_tensor("PEb", [128, 512], F32).ap()
    PF = nc.alloc_psum_tensor("PF", [128, 512], F32).ap()
    PG = nc.alloc_psum_tensor("PG", [128, 512], F32).ap()

    pfree = {}
    dbg_guard = {}

    def pw(*keys):
        return [pfree[k] for k in keys if k in pfree]

    p.op("vector", lambda e: e.memset(sb_ones, 1.0))
    const_toks = []
    const_toks.append(p.dma("sync", lambda e: e.dma_start(
        out=sb_ones_bf, in_=c_ones2.ap())))
    p.op("vector", lambda e: e.memset(sb_nc63, -63.0 * 0.8), inc=vC)
    for (dstap, srct) in ((sb_ident, c_ident), (sb_mask, c_mask),
                          (sb_lcol, c_lcol)):
        const_toks.append(p.dma("sync", lambda e, d=dstap, s=srct: e.dma_start(
            out=d, in_=s.ap())))
    const_toks.append(p.dma("sync", lambda e: e.dma_start(out=sb_u, in_=d_u)))
    for m in "rkvw":
        const_toks.append(p.dma("sync", lambda e, m=m: e.dma_start(
            out=sb_tm[m], in_=d_tm[m])))
    const_toks.append(p.dma("sync", lambda e: e.dma_start(
        out=sb_gw.rearrange("p (f e) -> p f e", f=FC),
        in_=d_gw.rearrange("(f p) e -> p f e", p=128))))
    const_toks.append(p.dma("sync", lambda e: e.dma_start(
        out=sb_eb[:, 0:1], in_=d_eb[:, None])))
    t_ones = p.op("vector", lambda e: e.engine_nop(), waits=const_toks, inc=vC)
    t_const = t_ones

    estA = ExitStack()
    AS = lambda name, shape, dt=F32: estA.enter_context(
        nc.sbuf_tensor(name, shape, dt)).ap()
    sb_W = {m: AS(f"W{m}_bf", [128, FC * D], BF16) for m in "rkvwo"}
    sb_wstage = AS("wstage", [128, 1024])
    sb_lnwbc = AS("lnwbc", [128, D])
    sb_lnbbc = AS("lnbbc", [128, D])
    sb_lnrow = AS("lnrow", [1, D])
    sb_xt = AS("xt", [128, FC * TT])
    sb_xsq = AS("xsq", [128, FC * TT])
    sb_xn = AS("xn_ext", [128, FC * (TT + 1)])
    sb_row = AS("rows", [1, 4 * TT])
    sb_delta = sb_xt
    sb_tmd = AS("tmd2", [128, 2 * TT])
    sb_xmix = [AS(f"xmix{i}", [128, FC * TT], BF16) for i in range(2)]
    sb_r = AS("r_f32", [128, FC * TT])
    sb_k = AS("k_f32", [128, FC * TT])
    sb_w = AS("w_f32", [128, FC * TT])
    sb_wtmp = AS("wtmp", [128, TT])
    sb_v = AS("v_tm", [128, CPT * D], BF16)
    sb_V = AS("Vcum", [128, FC * C])
    sb_excl = AS("excl", [128, FC * C])
    sb_E = AS("Ebuf", [128, 8 * C])
    sb_mcol = AS("mcol", [128, 3 * FC])
    sb_decay = AS("decay", [128, 2 * FC])
    sb_Rp = AS("Rp_f32", [128, C])
    sb_Rpbf = AS("Rp_bf", [128, FC * C], BF16)
    sb_Rbf = AS("R_bf", [128, FC * C], BF16)
    sb_Kpad = AS("Kpad", [128, H * C], BF16)
    sb_Khat = AS("Khat", [128, C])
    sb_Khtm = AS("Khtm", [128, FC * C], BF16)
    sb_B = AS("B_bf", [128, C], BF16)
    sb_Am = AS("Amask", [128, 4 * C], BF16)
    sb_d = AS("d_tm", [128, H])
    sb_otm = AS("o_tm", [128, D])
    sb_osq = sb_xsq[:, 0:D]
    sb_ost = AS("ostats", [128, 6 * H])
    sb_ob = sb_xsq[:, D:2 * D]
    sb_obfm = AS("obar_fm", [128, FC * TT], BF16)
    sb_S = AS("S_f32", [128, H * DH])
    sb_Sbf = AS("S_bf", [128, H * DH], BF16)
    sb_sc = AS("scA", [NE, TT])
    sb_bia = AS("scB", [NE, TT])
    sb_btm = AS("b_tm", [128, NE])
    sb_stm = AS("s_tm", [128, NE])
    sb_cmp = AS("cmp8", [128, NE])
    sb_gt3 = AS("gt3", [128, NE * NE])
    sb_eq3 = AS("eq3", [128, NE * NE])
    sb_rkA = AS("rkA", [128, NE])
    sb_rkB = AS("rkB", [128, NE])
    sb_sel8 = AS("sel8", [128, NE])
    sb_rank = AS("rank", [128, 4])
    sb_srow = AS("srow", [NE, TT])

    p.op("vector", lambda e: e.memset(sb_rank[:, 3:4], 2.0))
    p.op("vector", lambda e: e.memset(sb_eq3, 0.0), inc=vC)
    p.op("vector", lambda e: e.memset(sb_S, 0.0))
    p.op("vector", lambda e: e.memset(sb_Sbf, 0.0))
    p.op("vector", lambda e: e.memset(sb_Kpad, 0.0))
    t_carry0 = p.op("vector", lambda e: e.memset(sb_xn, 0.0), inc=vC)

    wc_tok = {}
    for m in "rkvwo":
        wc_tok[m] = p.dma("gpsimd", lambda e, m=m: e.dma_start(
            out=sb_W[m].rearrange("p (c d) -> p c d", c=8),
            in_=d_W[m].rearrange("(c p) d -> p c d", p=128)))
    t_stage_free = wc_tok["o"]
    sb_Rp2 = sb_wstage[:, 0:C]
    sb_rampinv = sb_wstage[:, 128:256]
    sb_mask4 = sb_wstage[:, 384:896]
    t_rampld = p.dma("sync", lambda e: e.dma_start(
        out=sb_rampinv, in_=c_rampinv.ap()), waits=[t_stage_free])
    t_rampld2 = p.dma("sync", lambda e: e.dma_start(
        out=sb_mask4, in_=c_mask4.ap()), waits=[t_stage_free])

    lt = p.dma("sync", lambda e: e.dma_start(out=sb_lnrow[0:1, 0:D], in_=d_lnw[None, :]))
    b1 = p.op("tensor", lambda e: e.matmul(PC0, sb_ones[0:1, 0:128],
                                           sb_lnrow[0:1, 0:512], start=True, stop=True),
              waits=[lt, t_ones], inc=tC)
    b2 = p.op("tensor", lambda e: e.matmul(PC1, sb_ones[0:1, 0:128],
                                           sb_lnrow[0:1, 512:1024], start=True, stop=True),
              waits=[lt], inc=tC)
    c1 = p.op("vector", lambda e: e.tensor_copy(sb_lnwbc, PCfull),
              waits=[b1, b2], inc=vC)
    lt2 = p.dma("sync", lambda e: e.dma_start(out=sb_lnrow[0:1, 0:D], in_=d_lnb[None, :]),
               waits=[c1])
    b3 = p.op("tensor", lambda e: e.matmul(PC0, sb_ones[0:1, 0:128],
                                           sb_lnrow[0:1, 0:512], start=True, stop=True),
              waits=[lt2, c1], inc=tC)
    b4 = p.op("tensor", lambda e: e.matmul(PC1, sb_ones[0:1, 0:128],
                                           sb_lnrow[0:1, 512:1024],
                                           start=True, stop=True),
              waits=[lt2], inc=tC)
    t_lnb = p.op("vector", lambda e: e.tensor_copy(sb_lnbbc, PCfull),
                 waits=[b3, b4], inc=vC)
    pfree["PC0"] = t_lnb
    pfree["PC1"] = t_lnb

    t_srw_tail = None
    t_build = None
    t_kb = None
    t_ktr_tr = None
    t_dmm = None
    prev_kb_hist = None
    t_su0_prev = None
    t_obf_last = None
    t_xt_free = None
    t_xnbf_free = None
    t_srow_free = None
    t_stile_free = None
    t_buf_free = {0: None, 1: None}
    t_chunk_tensor_done = None
    t_headmm_done = None
    t_obtr_last = None
    t_wo_prev = None
    t_w_free = None
    t_xsq_free = None
    t_k_free = None
    t_r_free = None
    t_v_free = None

    x3 = d_x.rearrange("(f pp) t -> pp f t", pp=128)
    xt3 = sb_xt.rearrange("pp (f t) -> pp f t", f=FC)

    for ti in range(NT):
        t0 = ti * TT

        t_xload = p.dma("sync", lambda e, t0=t0: e.dma_start(
            out=xt3, in_=x3[:, :, t0:t0 + TT]),
            waits=[t_xt_free] if t_xt_free else [])

        if ti > 0:
            def carry(e):
                for fc in range(FC):
                    base = fc * (TT + 1)
                    ins = e.tensor_copy(sb_xn[:, base:base + 1],
                                        sb_xn[:, base + TT:base + TT + 1])
                return ins
            t_carry = p.op("vector", carry, inc=vC)
        else:
            t_carry = t_carry0

        sqw = [t_xload]
        if t_obtr_last is not None:
            sqw.append(t_obtr_last)
        if t_xsq_free is not None:
            sqw.append(t_xsq_free)
        t_sq = p.op("vector", lambda e: e.tensor_mul(sb_xsq, sb_xt, sb_xt),
                    waits=sqw, inc=vC)

        def stats(e, off, srcap):
            for fc in range(FC):
                ins = e.matmul(PD[0:1, off:off + TT], sb_ones[:, 0:1],
                               srcap[:, fc * TT:(fc + 1) * TT],
                               start=(fc == 0), stop=(fc == FC - 1))
            return ins
        t_st1 = p.op("tensor", lambda e: stats(e, 0, sb_xt),
                     waits=[t_xload, t_ones] + pw("PD"), inc=tC)
        t_st2 = p.op("tensor", lambda e: stats(e, TT, sb_xsq), waits=[t_sq], inc=tC)

        t_mu = p.op("vector", lambda e: e.tensor_scalar(
            sb_row[0:1, 0:TT], PD[0:1, 0:TT], 1.0 / D, None, ALU.mult),
            waits=[t_st2], inc=vC)
        t_e2 = p.op("vector", lambda e: e.tensor_scalar(
            sb_row[0:1, TT:2 * TT], PD[0:1, TT:2 * TT], 1.0 / D, None, ALU.mult),
            inc=vC)
        pfree["PD"] = t_e2
        t_m2 = p.op("vector", lambda e: e.tensor_mul(
            sb_row[0:1, 2 * TT:3 * TT], sb_row[0:1, 0:TT], sb_row[0:1, 0:TT]),
            waits=[t_e2], inc=vC)
        t_m3 = p.op("vector", lambda e: e.tensor_sub(
            sb_row[0:1, 2 * TT:3 * TT], sb_row[0:1, TT:2 * TT],
            sb_row[0:1, 2 * TT:3 * TT]), waits=[t_m2], inc=vC)
        t_var = p.op("vector", lambda e: e.tensor_scalar(
            sb_row[0:1, 2 * TT:3 * TT], sb_row[0:1, 2 * TT:3 * TT],
            EPS, None, ALU.add), waits=[t_m3], inc=vC)
        t_ln1 = p.op("scalar", lambda e: e.activation(sb_row[0:1, 2 * TT:3 * TT],
                                                      sb_row[0:1, 2 * TT:3 * TT],
                                                      AF.Ln), waits=[t_var], inc=sC)
        t_rho = p.op("scalar", lambda e: e.activation(sb_row[0:1, TT:2 * TT],
                                                      sb_row[0:1, 2 * TT:3 * TT],
                                                      AF.Exp, scale=-0.5),
                     waits=[t_ln1], inc=sC)
        t_mr = p.op("vector", lambda e: e.tensor_mul(sb_row[0:1, 3 * TT:4 * TT],
                                                     sb_row[0:1, 0:TT],
                                                     sb_row[0:1, TT:2 * TT]),
                    waits=[t_rho, t_mu], inc=vC)
        t_bc1 = p.op("tensor", lambda e: e.matmul(PA[:, 0:TT], sb_ones[0:1, 0:128],
                                                  sb_row[0:1, TT:2 * TT],
                                                  start=True, stop=True),
                     waits=[t_mr] + pw("PA"), inc=tC)
        t_bc2 = p.op("tensor", lambda e: e.matmul(PB[:, 0:TT], sb_ones[0:1, 0:128],
                                                  sb_row[0:1, 3 * TT:4 * TT],
                                                  start=True, stop=True),
                     waits=[t_mr] + pw("PB"), inc=tC)

        def norm0(e):
            for fc in range(FC):
                ins = e.tensor_mul(sb_xsq[:, fc * TT:(fc + 1) * TT],
                                   sb_xt[:, fc * TT:(fc + 1) * TT], PA[:, 0:TT])
            return ins

        def norm(e):
            for fc in range(FC):
                base = fc * (TT + 1)
                ins = e.tensor_sub(sb_xn[:, base + 1:base + 1 + TT],
                                   sb_xsq[:, fc * TT:(fc + 1) * TT], PB[:, 0:TT])
            return ins
        t_xn0 = p.op("vector", norm0,
                     waits=[t_bc1, t_bc2, t_carry] +
                           ([t_xnbf_free] if t_xnbf_free else []), inc=vC)
        t_xn = p.op("vector", norm, waits=[t_xn0], inc=vC)
        pfree["PA"] = t_xn
        pfree["PB"] = t_xn
        t_xt_free = t_xn

        t_xnbf_free = p.dma("gpsimd", lambda e, t0=t0: e.dma_start(
            out=d_xnbf[:, :, t0:t0 + TT],
            in_=sb_xn.rearrange("pp (f t) -> pp f t", f=FC, t=TT + 1)[:, :, 1:TT + 1]),
            waits=[t_xn],)
        if "xn" in d_dbg:
            p.dma("gpsimd", lambda e, t0=t0: e.dma_start(
                out=d_dbg["xn"][:, :, t0:t0 + TT],
                in_=sb_xn.rearrange("pp (f t) -> pp f t", f=FC, t=TT + 1)[:, :, 1:TT + 1]),
                waits=[t_xn],)

        def gmm(e):
            for fc in range(FC):
                base = fc * (TT + 1)
                ins = e.matmul(PD[0:NE, 0:TT], sb_gw[:, fc * NE:(fc + 1) * NE],
                               sb_xn[:, base + 1:base + 1 + TT],
                               start=(fc == 0), stop=(fc == FC - 1))
            return ins
        t_g = p.op("tensor", gmm, waits=[t_xn, t_const] + pw("PD"), inc=tC)
        t_ge = p.op("scalar", lambda e: e.activation(sb_bia[0:NE, :], PD[0:NE, 0:TT],
                                                     AF.Exp, scale=-1.0),
                    waits=[t_g], inc=sC)
        t_pd_done = p.op("scalar", lambda e: e.activation(
            sb_sc[0:NE, :], sb_bia[0:NE, :], AF.Ln, bias=1.0),
            waits=[t_ge], inc=sC)
        pfree["PD"] = t_pd_done
        t_sc1 = p.op("scalar", lambda e: e.activation(
            sb_bia[0:NE, :], sb_sc[0:NE, :], AF.Exp, scale=-1.0),
            waits=[t_pd_done], inc=sC)
        t_bia = p.op("vector", lambda e: e.tensor_scalar(
            sb_sc[0:NE, :], sb_bia[0:NE, :], sb_eb[0:NE, 0:1], None, ALU.add),
            waits=[t_sc1], inc=vC)
        if "scores" in d_dbg:
            p.dma("gpsimd", lambda e, t0=t0: e.dma_start(
                out=d_dbg["scores"][:, t0:t0 + TT], in_=sb_bia[0:NE, :]),
                waits=[t_bia],)

        t_srw = None
        for cc in range(CPT):
            t_srw_prev = t_srw if cc > 0 else t_srw_tail
            co = cc * C
            tb = p.op("tensor", lambda e, co=co: e.transpose(
                PG[:, 0:NE], sb_sc[0:NE, co:co + C], sb_ident[0:NE, 0:NE]),
                waits=[t_bia] + pw("PG"), inc=tC)
            tb2 = p.op("tensor", lambda e, co=co: e.transpose(
                PG[:, 8:8 + NE], sb_bia[0:NE, co:co + C], sb_ident[0:NE, 0:NE]),
                inc=tC)
            tv = p.op("vector", lambda e: e.tensor_copy(sb_btm, PG[:, 0:NE]),
                      waits=[tb, tb2] + ([t_srw_prev] if t_srw_prev else []) +
                            ([dbg_guard["btm"]] if "btm" in dbg_guard else []),
                      inc=vC)
            tv2 = p.op("vector", lambda e: e.tensor_copy(sb_stm, PG[:, 8:8 + NE]),
                       inc=vC)
            pfree["PG"] = tv2

            def select_w(e):
                for j in range(NE):
                    e.tensor_scalar(sb_gt3[:, j * NE:(j + 1) * NE], sb_btm,
                                    sb_btm[:, j:j + 1], None, ALU.is_gt)
                for j in range(1, NE):
                    ins = e.tensor_scalar(sb_eq3[:, j * NE:j * NE + j],
                                          sb_btm[:, 0:j], sb_btm[:, j:j + 1], None,
                                          ALU.is_equal)
                return ins

            def select(e):
                e.tensor_reduce(sb_rkA.rearrange("p (a o) -> p a o", o=1),
                                sb_gt3.rearrange("p (a b) -> p a b", a=NE),
                                mybir.AxisListType.X, ALU.add)
                ins = e.tensor_reduce(sb_rkB.rearrange("p (a o) -> p a o", o=1),
                                      sb_eq3.rearrange("p (a b) -> p a b", a=NE),
                                      mybir.AxisListType.X, ALU.add)
                return ins
            if "btm" in d_dbg:
                p.dma("gpsimd", lambda e, ti=ti, cc=cc: e.dma_start(
                    out=d_dbg["btm"][ti * CPT + cc], in_=sb_btm),
                    waits=[tv],)
                dbg_guard["btm"] = p.dma("gpsimd", lambda e, ti=ti, cc=cc: e.dma_start(
                    out=d_dbg["stm"][ti * CPT + cc], in_=sb_stm),
                    waits=[tv2],)
            selw = [tv, tv2]
            for nm in ("gt3", "sel8"):
                if nm in dbg_guard:
                    selw.append(dbg_guard[nm])
            t_selw = p.op("vector", select_w, waits=selw, inc=vC)
            t_sel0 = p.op("vector", select, waits=[t_selw], inc=vC)

            t_sA = p.op("vector", lambda e: e.tensor_add(sb_cmp, sb_rkA, sb_rkB),
                        waits=[t_sel0], inc=vC)
            t_sB = p.op("vector", lambda e: e.tensor_scalar(
                sb_sel8, sb_cmp, sb_ones[:, 0:1], None, ALU.is_le),
                waits=[t_sA], inc=vC)
            t_sel = p.op("vector", lambda e: e.tensor_mul(sb_sel8, sb_sel8, sb_stm),
                        waits=[t_sB], inc=vC)
            if "gt3" in d_dbg:
                for nm, src in (("gt3", sb_gt3), ("eq3", sb_eq3), ("rkA", sb_rkA),
                                ("rkB", sb_rkB), ("rkT", sb_cmp)):
                    dbg_guard["gt3"] = p.dma(
                        "gpsimd", lambda e, ti=ti, cc=cc, nm=nm, src=src:
                        e.dma_start(out=d_dbg[nm][ti * CPT + cc], in_=src),
                        waits=[t_sel])
            if "sel8" in d_dbg:
                dbg_guard["sel8"] = p.dma("gpsimd", lambda e, ti=ti, cc=cc: e.dma_start(
                    out=d_dbg["sel8"][ti * CPT + cc], in_=sb_sel8),
                    waits=[t_sel],)
            tbk = p.op("tensor", lambda e: e.transpose(
                PG[0:NE, 0:128], sb_sel8, sb_ident),
                waits=[t_sel] + pw("PG"), inc=tC)
            sw = [tbk]
            if t_srow_free is not None and cc == 0:
                sw.append(t_srow_free)
            t_srw = p.op("vector", lambda e, co=co: e.tensor_copy(
                sb_srow[0:NE, co:co + C], PG[0:NE, 0:128]), waits=sw, inc=vC)
            pfree["PG"] = t_srw
        t_srw_tail = t_srw
        t_srow_free = p.dma("gpsimd", lambda e, t0=t0: e.dma_start(
            out=d_sel[:, t0:t0 + TT], in_=sb_srow[0:NE, :]),
            waits=[t_srw],)
        if "sel" in d_dbg:
            p.dma("gpsimd", lambda e, t0=t0: e.dma_start(
                out=d_dbg["sel"][:, t0:t0 + TT], in_=sb_srow[0:NE, :]),
                waits=[t_srw],)

        proj_tok = {}
        wvts = {}
        t_wtmp_rd = None

        def mkdelta(e):
            for fc in range(FC):
                base = fc * (TT + 1)
                ins = e.tensor_sub(sb_delta[:, fc * TT:(fc + 1) * TT],
                                   sb_xn[:, base + 1:base + 1 + TT],
                                   sb_xn[:, base:base + TT])
            return ins
        t_delta = p.op("vector", mkdelta, waits=[t_xn], inc=vC)
        tmd_tok = {}
        for mi, m in enumerate("wkvr"):
            buf = sb_xmix[mi % 2]
            t_mix = None
            for fc in range(FC):
                slot = (mi * FC + fc) % 2
                tsw = [t_delta]
                if tmd_tok.get(slot - 2 if slot >= 2 else (mi * FC + fc) - 2):
                    pass
                prev = tmd_tok.get((mi * FC + fc) - 2)
                ts_t = p.op("gpsimd", lambda e, m=m, fc=fc, slot=slot:
                            e.tensor_scalar(
                    sb_tmd[:, slot * TT:(slot + 1) * TT],
                    sb_delta[:, fc * TT:(fc + 1) * TT],
                    sb_tm[m][:, fc:fc + 1], None, ALU.mult),
                    waits=[t_delta] + ([prev] if prev else []), inc=gC)
                base = fc * (TT + 1)
                t_mix = p.op("vector", lambda e, fc=fc, slot=slot, buf=buf,
                             base=base: e.tensor_add(
                                 buf[:, fc * TT:(fc + 1) * TT],
                                 sb_tmd[:, slot * TT:(slot + 1) * TT],
                                 sb_xn[:, base:base + TT]),
                             waits=[ts_t] +
                                   ([t_buf_free[mi % 2]] if (fc == 0 and
                                    t_buf_free[mi % 2] is not None) else []),
                             inc=vC)
                tmd_tok[mi * FC + fc] = t_mix
            t_xt_free = t_mix

            if m == "v":
                vro = [t_v_free] if t_v_free else []
                for cc in range(CPT):
                    def vproj(e, buf=buf, cc=cc):
                        for half, ps in ((0, PC0), (1, PC1)):
                            for kc in range(FC):
                                ins = e.matmul(
                                    ps,
                                    buf[:, kc * TT + cc * C:kc * TT + cc * C + C],
                                    sb_W["v"][:, kc * D + half * 512:
                                              kc * D + half * 512 + 512],
                                    start=(kc == 0), stop=(kc == FC - 1))
                        return ins
                    tmm = p.op("tensor", vproj,
                               waits=[t_mix, wc_tok["v"]] + pw("PC0", "PC1"),
                               inc=tC)
                    p.op("scalar", lambda e, cc=cc: e.activation(
                        sb_v[:, cc * D:cc * D + 512], PC0, AF.Copy),
                        waits=[tmm] + (vro if cc == 0 else []), inc=sC)
                    tev = p.op("scalar", lambda e, cc=cc: e.activation(
                        sb_v[:, cc * D + 512:(cc + 1) * D], PC1, AF.Copy),
                        inc=sC)
                    pfree["PC0"] = tev
                    pfree["PC1"] = tev
                    proj_tok[("v", cc)] = tev
                    if "vtm" in d_dbg:
                        p.dma("gpsimd", lambda e, cc=cc, ti=ti: e.dma_start(
                            out=d_dbg["vtm"][ti * CPT + cc],
                            in_=sb_v[:, cc * D:(cc + 1) * D]),
                            waits=[tev],)
                t_buf_free[mi % 2] = tmm
                continue

            last_ev = None
            for mc in range(FC):
                ps, key = (PA, "PA") if mc % 2 == 0 else (PB, "PB")

                def fproj(e, buf=buf, m=m, mc=mc, ps=ps):
                    for kc in range(FC):
                        ins = e.matmul(
                            ps[:, 0:TT],
                            sb_W[m][:, kc * D + mc * 128:kc * D + mc * 128 + 128],
                            buf[:, kc * TT:(kc + 1) * TT],
                            start=(kc == 0), stop=(kc == FC - 1))
                    return ins
                tmm = p.op("tensor", fproj,
                           waits=[t_mix, wc_tok[m]] + pw(key), inc=tC)
                sl0, sl1 = mc * TT, (mc + 1) * TT
                if m == "k":
                    kw = [tmm] + ([t_k_free] if (t_k_free and mc == 0) else [])
                    last_ev = p.op("vector", lambda e, sl0=sl0, sl1=sl1, ps=ps:
                                   e.tensor_copy(sb_k[:, sl0:sl1], ps[:, 0:TT]),
                                   waits=kw, inc=vC)
                elif m == "w":
                    if "zw" in d_dbg:
                        zsl = (mc % 4) * TT
                        tzc = p.op("vector", lambda e, ps=ps, zsl=zsl: e.tensor_copy(
                            sb_excl[:, zsl:zsl + TT], ps[:, 0:TT]),
                            waits=[tmm], inc=vC)
                        p.dma("gpsimd", lambda e, mc=mc, t0=t0, zsl=zsl: e.dma_start(
                            out=d_dbg["zw"][:, mc, t0:t0 + TT],
                            in_=sb_excl[:, zsl:zsl + TT]), waits=[tzc],)
                        tmm = tzc
                    ww = [tmm] + ([t_w_free] if (t_w_free and mc == 0) else [])
                    if wvts.get(mc - 2) is not None:
                        ww.append(wvts[mc - 2])
                    vslot = sb_V[:, (mc % 2) * TT:(mc % 2) * TT + TT]
                    if t_wtmp_rd is not None:
                        ww.append(t_wtmp_rd)
                    t_we = p.op("scalar", lambda e, ps=ps: e.activation(
                        sb_wtmp, ps[:, 0:TT], AF.Exp, scale=-1.0), waits=ww,
                        inc=sC)
                    t_wln = p.op("scalar", lambda e, vs=vslot: e.activation(
                        vs, sb_wtmp, AF.Ln, bias=1.0), waits=[t_we], inc=sC)
                    t_wtmp_rd = t_wln
                    last_ev = p.op("vector", lambda e, sl0=sl0, sl1=sl1, vs=vslot:
                                   e.tensor_scalar(sb_w[:, sl0:sl1], vs,
                                                   -1.0, -1e-4, ALU.mult, ALU.add),
                                   waits=[t_wln], inc=vC)
                    wvts[mc] = last_ev
                else:
                    rw = [tmm] + (t_r_free if (t_r_free and mc == 0) else [])
                    if t_wtmp_rd is not None:
                        rw.append(t_wtmp_rd)
                    t_re = p.op("scalar", lambda e, ps=ps: e.activation(
                        sb_wtmp, ps[:, 0:TT], AF.Exp, scale=-1.0), waits=rw,
                        inc=sC)
                    t_rl = p.op("scalar", lambda e, sl0=sl0, sl1=sl1: e.activation(
                        sb_xsq[:, sl0:sl1], sb_wtmp, AF.Ln, bias=1.0),
                        waits=[t_re], inc=sC)
                    t_wtmp_rd = t_rl
                    last_ev = p.op("scalar", lambda e, sl0=sl0, sl1=sl1: e.activation(
                        sb_r[:, sl0:sl1], sb_xsq[:, sl0:sl1], AF.Exp, scale=-1.0),
                        waits=[t_rl], inc=sC)
                pfree[key] = last_ev
            proj_tok[m] = last_ev
            t_buf_free[mi % 2] = tmm

        for nm in ("w", "k", "r"):
            if nm in d_dbg:
                srcmap = {"w": sb_w, "k": sb_k, "r": sb_r}
                p.dma("gpsimd", lambda e, nm=nm, t0=t0: e.dma_start(
                    out=d_dbg[nm][:, :, t0:t0 + TT],
                    in_=srcmap[nm].rearrange("pp (f t) -> pp f t", f=FC)),
                    waits=[proj_tok[nm]],)

        if stop_after == "proj":
            t_w_free = proj_tok["w"]
            t_k_free = proj_tok["k"]
            t_r_free = [proj_tok["r"]]
            t_v_free = proj_tok[("v", CPT - 1)]
            t_xsq_free = proj_tok["r"]
            continue

        ch = {}

        def emitA(cc):
            nonlocal t_build, t_kb, t_ktr_tr, t_dmm, t_headmm_done, prev_kb_hist
            st = {}
            co = cc * C

            def prep1a0(e, co=co):
                for fc in range(FC):
                    wsl = slice(fc * TT + co, fc * TT + co + C)
                    ins = e.tensor_scalar(sb_excl[:, fc * C:(fc + 1) * C],
                                          sb_w[:, wsl], 0.8, None, ALU.add)
                return ins

            def prep1a(e):
                for fc in range(FC):
                    ins = e.tensor_tensor_scan(sb_V[:, fc * C:(fc + 1) * C],
                                               sb_ones[:, 0:C],
                                               sb_excl[:, fc * C:(fc + 1) * C],
                                               0.0, ALU.mult, ALU.add)
                return ins

            def prep1(e):
                for fc in range(FC):
                    e.tensor_sub(sb_excl[:, fc * C:(fc + 1) * C],
                                 sb_V[:, fc * C:(fc + 1) * C],
                                 sb_excl[:, fc * C:(fc + 1) * C])
                    e.tensor_scalar(sb_mcol[:, fc:fc + 1],
                                    sb_V[:, fc * C + 63:fc * C + 64],
                                    -1.0, None, ALU.mult)
                    e.tensor_scalar(sb_mcol[:, FC + fc:FC + fc + 1],
                                    sb_V[:, fc * C + 127:fc * C + 128],
                                    -64.0 * 0.8, None, ALU.add)
                    ins = e.tensor_scalar(sb_mcol[:, 2 * FC + fc:2 * FC + fc + 1],
                                          sb_V[:, fc * C + 127:fc * C + 128],
                                          -128.0 * 0.8, None, ALU.add)
                return ins
            prw = [proj_tok["w"]]
            if ti == 0 and cc == 0:
                prw.append(t_rampld)
                prw.append(t_rampld2)
            if t_kb is not None:
                prw.append(t_kb)
            t_prep1a0 = p.op("vector", prep1a0, waits=prw, inc=vC)
            t_prep1a = p.op("vector", prep1a, waits=[t_prep1a0], inc=vC)
            t_prep1 = p.op("vector", prep1, waits=[t_prep1a], inc=vC)
            st["t_prep1"] = t_prep1

            def cols(e):
                for fc in range(FC):
                    e.activation(sb_decay[:, fc:fc + 1],
                                 sb_mcol[:, 2 * FC + fc:2 * FC + fc + 1],
                                 AF.Exp)
                    ins = e.activation(sb_decay[:, FC + fc:FC + fc + 1],
                                       sb_mcol[:, 2 * FC + fc:2 * FC + fc + 1],
                                       AF.Exp)
                return ins
            t_cols = p.op("scalar", cols,
                          waits=[t_prep1] +
                                ([t_su0_prev] if t_su0_prev else []), inc=sC)

            t_build_hist = {}
            t_kb_hist = {}
            t_ktr_ev = None
            for fc in range(FC):
                eo = (fc % 2) * 4 * C

                def eops(e, fc=fc, co=co, eo=eo):
                    e.activation(sb_E[:, eo:eo + C],
                                 sb_excl[:, fc * C:(fc + 1) * C], AF.Exp,
                                 bias=sb_mcol[:, fc:fc + 1])
                    e.activation(sb_E[:, eo + C:eo + 2 * C],
                                 sb_V[:, fc * C:(fc + 1) * C],
                                 AF.Exp, scale=-1.0,
                                 bias=sb_V[:, fc * C + 63:fc * C + 64])
                    e.activation(sb_E[:, eo + 2 * C:eo + 3 * C],
                                 sb_V[:, fc * C:(fc + 1) * C],
                                 AF.Exp, scale=-1.0,
                                 bias=sb_mcol[:, FC + fc:FC + fc + 1])
                    e.activation(sb_w[:, fc * TT + co:fc * TT + co + C],
                                 sb_excl[:, fc * C:(fc + 1) * C], AF.Exp,
                                 bias=sb_nc63[:, 0:1])
                    ins = e.activation(sb_E[:, eo + 3 * C:eo + 4 * C],
                                       sb_k[:, fc * TT + co:fc * TT + co + C],
                                       AF.Exp, bias=sb_u[:, fc:fc + 1])
                    return ins
                ew = [t_prep1]
                if fc == 0:
                    ew.append(proj_tok["k"])
                if t_build_hist.get(fc - 2):
                    ew.append(t_build_hist[fc - 2])
                if t_kb_hist.get(fc - 2):
                    ew.append(t_kb_hist[fc - 2])
                if fc < 2 and prev_kb_hist is not None:
                    ew.append(prev_kb_hist[fc + 6])
                t_eops = p.op("scalar", eops, waits=ew, inc=sC)

                def buildf0(e, fc=fc, co=co, eo=eo):
                    ksl = slice(fc * TT + co, fc * TT + co + C)
                    e.tensor_mul(sb_Rp2, sb_r[:, ksl], sb_w[:, ksl])
                    return e.tensor_mul(sb_Rp, sb_r[:, ksl], sb_E[:, eo:eo + C])

                def buildf(e, fc=fc):
                    e.tensor_mul(sb_Rpbf[:, fc * C:(fc + 1) * C], sb_Rp2,
                                 sb_rampinv)
                    ins = e.tensor_copy(sb_Rbf[:, fc * C:(fc + 1) * C], sb_Rp)
                    return ins

                def buildk(e, fc=fc, co=co, eo=eo):
                    h0, h1 = 2 * fc, 2 * fc + 1
                    ksl = slice(fc * TT + co, fc * TT + co + C)
                    e.tensor_tensor(sb_Kpad[0:64, h0 * C:(h0 + 1) * C],
                                    sb_k[0:64, ksl], sb_E[0:64, eo + C:eo + 2 * C],
                                    ALU.mult)
                    e.tensor_tensor(sb_Kpad[64:128, h1 * C:(h1 + 1) * C],
                                    sb_k[64:128, ksl],
                                    sb_E[64:128, eo + C:eo + 2 * C], ALU.mult)
                    e.tensor_mul(sb_Khat, sb_k[:, ksl],
                                 sb_E[:, eo + 2 * C:eo + 3 * C])
                    ins = e.tensor_mul(sb_B, sb_r[:, ksl],
                                       sb_E[:, eo + 3 * C:eo + 4 * C])
                    return ins
                b0w = [t_eops, t_cols]
                if fc == 0:
                    b0w.append(proj_tok["r"])
                if t_build is not None:
                    b0w.append(t_build)
                if fc == 0 and t_headmm_done is not None:
                    b0w.append(t_headmm_done)
                t_build0 = p.op("vector", buildf0, waits=b0w, inc=vC)
                t_build = p.op("vector", buildf, waits=[t_build0], inc=vC)
                t_build_hist[fc] = t_build
                kbw = [t_eops, t_cols]
                if fc == 0:
                    kbw.append(proj_tok["r"])
                if t_ktr_tr is not None:
                    kbw.append(t_ktr_tr)
                if t_dmm is not None:
                    kbw.append(t_dmm)
                if fc == 0 and t_headmm_done is not None:
                    kbw.append(t_headmm_done)
                t_kb = p.op("gpsimd", buildk, waits=kbw, inc=gC)
                t_kb_hist[fc] = t_kb

                ktr_ps = PEb if fc % 2 == 0 else PF
                ktr_key = "PEb" if fc % 2 == 0 else "PF"
                tt1 = p.op("tensor", lambda e, ps=ktr_ps: e.transpose(
                    ps[:, 0:128], sb_Khat, sb_ident),
                    waits=[t_kb] + pw(ktr_key), inc=tC)
                t_ktr_tr = tt1
                t_ktr_ev = p.op("vector", lambda e, fc=fc, ps=ktr_ps: e.tensor_scalar(
                    sb_Khtm[:, fc * C:(fc + 1) * C], ps[:, 0:128],
                    sb_lcol[:, 0:1], None, ALU.mult),
                    waits=[tt1], inc=vC)
                pfree[ktr_key] = t_ktr_ev
                t_dmm = p.op("tensor", lambda e, fc=fc: e.matmul(
                    PD[:, 128 + 2 * fc:128 + 2 * fc + 2], sb_B, sb_ones_bf,
                    start=True, stop=True),
                    waits=[t_kb] + (pw("PD") if fc == 0 else []), inc=tC)
            t_dev = p.op("vector", lambda e: e.tensor_copy(sb_d, PD[:, 128:128 + H]),
                         waits=[t_dmm], inc=vC)
            pfree["PD"] = t_dev
            st["t_build_hist"] = t_build_hist
            st["t_kb_hist"] = t_kb_hist
            st["t_ktr_ev"] = t_ktr_ev
            st["t_dev"] = t_dev
            prev_kb_hist = t_kb_hist
            return st

        def emitB(cc, st):
            nonlocal t_headmm_done, t_su0_prev, t_chunk_tensor_done
            scol = lambda h: (h % 2) * 512 + (h // 2) * 64
            t_build_hist = st["t_build_hist"]
            t_kb_hist = st["t_kb_hist"]
            t_ktr_ev = st["t_ktr_ev"]
            t_dev = st["t_dev"]
            last_tm2 = {}
            tam_hist = {}
            tm1_hist = {}
            tm3 = None
            tm4 = None

            def emit_tm1(g):
                a_ps, a_key = (PA, "PA") if g % 2 == 0 else (PB, "PB")
                tm1 = None
                for hi in range(4):
                    h = g * 4 + hi
                    fc = h // 2
                    tm1 = p.op("tensor", lambda e, h=h, fc=fc, hi=hi, ps=a_ps:
                               e.matmul(ps[:, hi * C:(hi + 1) * C],
                                        sb_Kpad[:, h * C:(h + 1) * C],
                                        sb_Rbf[:, fc * C:(fc + 1) * C],
                                        start=True, stop=True),
                               waits=([t_build_hist[2 * g + 1],
                                       t_kb_hist[2 * g + 1]] + pw(a_key))
                                     if hi == 0 else [],
                               inc=tC)
                tm1_hist[g] = tm1

            def emit_grp(g):
                nonlocal tm3, tm4, t_chunk_tensor_done
                a_ps, a_key = (PA, "PA") if g % 2 == 0 else (PB, "PB")
                tam = p.op("vector", lambda e, ps=a_ps: e.tensor_mul(
                    sb_Am, ps[:, 0:512], sb_mask4),
                    waits=[tm1_hist[g]] +
                          ([last_tm2[g - 1]] if g - 1 in last_tm2 else []),
                    inc=vC)
                pfree[a_key] = tam
                tam_hist[g] = tam
                for hi in range(4):
                    h = g * 4 + hi
                    fc = h // 2
                    tm2 = p.op("tensor", lambda e, h=h, hi=hi, cc=cc: e.matmul(
                        PCfull[:, h * DH:(h + 1) * DH],
                        sb_Am[:, hi * C:(hi + 1) * C],
                        sb_v[:, cc * D + h * DH:cc * D + (h + 1) * DH],
                        start=True, stop=False),
                        waits=([tam] + (pw("PC0", "PC1") if h == 0 else []) +
                               ([proj_tok[("v", cc)]] if h == 0 else [])),
                        inc=tC)
                    tm3 = p.op("tensor", lambda e, h=h, fc=fc, scol=scol: e.matmul(
                        PCfull[:, h * DH:(h + 1) * DH],
                        sb_Rpbf[:, fc * C:(fc + 1) * C],
                        sb_Sbf[:, scol(h):scol(h) + DH],
                        start=False, stop=True), inc=tC)
                    d_ps, d_key = (PF, "PF") if h % 2 == 0 else (PG, "PG")
                    tm4 = p.op("tensor", lambda e, h=h, fc=fc, cc=cc, ps=d_ps:
                               e.matmul(ps[:, fc * DH:(fc + 1) * DH],
                                        sb_Khtm[:, fc * C:(fc + 1) * C],
                                        sb_v[:, cc * D + h * DH:cc * D + (h + 1) * DH],
                                        start=True, stop=True),
                               waits=([t_ktr_ev] + pw(d_key)) if h < 2 else [],
                               inc=tC)
                    last_tm2[g] = tm2

            emit_tm1(0)
            emit_tm1(1)
            emit_grp(0)
            emit_tm1(2)
            emit_grp(1)
            emit_tm1(3)
            emit_grp(2)
            emit_grp(3)
            t_chunk_tensor_done = tm4

            d_bc = sb_d.rearrange("pp (h o) -> pp h o", h=H).broadcast_to(
                [128, H, DH])
            v3c = sb_v[:, cc * D:(cc + 1) * D].rearrange(
                "pp (h j) -> pp h j", h=H)
            otm3v = sb_otm.rearrange("pp (h j) -> pp h j", h=H)
            t_oe0 = p.op("vector", lambda e, v3c=v3c: e.tensor_tensor(
                otm3v, v3c, d_bc, ALU.mult),
                waits=[t_dev] + ([dbg_guard["otm"]] if "otm" in dbg_guard else []),
                inc=vC)
            t_oev = p.op("vector", lambda e: e.tensor_add(
                sb_otm, sb_otm, PCfull[:, 0:D]), waits=[tm3, t_oe0], inc=vC)
            pfree["PC0"] = t_oev
            pfree["PC1"] = t_oev

            dec3 = sb_decay.rearrange("pp (f o) -> pp f o", f=2 * FC).broadcast_to(
                [128, 2 * FC, DH])
            S3 = sb_S.rearrange("pp (f j) -> pp f j", f=2 * FC)
            t_su0 = p.op("vector", lambda e: e.tensor_tensor(
                S3, S3, dec3, ALU.mult), waits=[tm4], inc=vC)
            t_su1a = p.op("vector", lambda e: e.tensor_add(
                sb_S[0:64, 0:512], sb_S[0:64, 0:512], PF[0:64, 0:512]),
                waits=[t_su0], inc=vC)
            pfree["PF"] = t_su1a
            t_su1b = p.op("vector", lambda e: e.tensor_add(
                sb_S[64:128, 512:1024], sb_S[64:128, 512:1024], PG[64:128, 0:512]),
                waits=[t_su1a], inc=vC)
            pfree["PG"] = t_su1b
            t_su = p.op("vector", lambda e: e.tensor_copy(sb_Sbf, sb_S),
                        waits=[t_su1b], inc=vC)
            t_headmm_done = tm4
            t_su0_prev = t_su0
            st["t_oev"] = t_oev
            st["t_su"] = t_su
            if "otm" in d_dbg:
                dbg_guard["otm"] = p.dma("gpsimd", lambda e, g=ti * CPT + cc:
                                         e.dma_start(out=d_dbg["otm"][g],
                                                     in_=sb_otm),
                                         waits=[t_oev],)

        def emitCgn(cc, st):
            t_oev = st["t_oev"]
            o3 = sb_otm.rearrange("pp (h j) -> pp h j", h=H)
            ob3 = sb_ob.rearrange("pp (h j) -> pp h j", h=H)
            oq3 = sb_osq.rearrange("pp (h j) -> pp h j", h=H)

            t_gsq = p.op("vector", lambda e: e.tensor_mul(sb_osq, sb_otm, sb_otm),
                         waits=[t_oev], inc=vC)

            def gn1(e):
                e.tensor_reduce(sb_ost[:, 0:H].rearrange("pp (h o) -> pp h o", h=H),
                                o3, mybir.AxisListType.X, ALU.add)
                return e.tensor_reduce(
                    sb_ost[:, H:2 * H].rearrange("pp (h o) -> pp h o", h=H),
                    oq3, mybir.AxisListType.X, ALU.add)
            t_gn1a = p.op("vector", gn1, waits=[t_gsq], inc=vC)

            t_gb1 = p.op("vector", lambda e: (
                e.tensor_scalar(sb_ost[:, 2 * H:3 * H], sb_ost[:, 0:H], 1.0 / DH,
                                None, ALU.mult),
                e.tensor_scalar(sb_ost[:, 3 * H:4 * H], sb_ost[:, H:2 * H],
                                1.0 / DH, None, ALU.mult))[-1],
                waits=[t_gn1a], inc=vC)
            t_gb2 = p.op("vector", lambda e: e.tensor_mul(
                sb_ost[:, 4 * H:5 * H], sb_ost[:, 2 * H:3 * H],
                sb_ost[:, 2 * H:3 * H]), waits=[t_gb1], inc=vC)
            t_gn1 = p.op("vector", lambda e: e.tensor_sub(
                sb_ost[:, 4 * H:5 * H], sb_ost[:, 3 * H:4 * H],
                sb_ost[:, 4 * H:5 * H]), waits=[t_gb2], inc=vC)

            t_gn1b = p.op("vector", lambda e: e.tensor_scalar(
                sb_ost[:, 4 * H:5 * H], sb_ost[:, 4 * H:5 * H], EPS, None,
                ALU.add), waits=[t_gn1], inc=vC)

            t_g2a = p.op("scalar", lambda e: e.activation(
                sb_ost[:, 4 * H:5 * H], sb_ost[:, 4 * H:5 * H], AF.Ln),
                waits=[t_gn1b], inc=sC)
            t_gn2 = p.op("scalar", lambda e: e.activation(
                sb_ost[:, 5 * H:6 * H], sb_ost[:, 4 * H:5 * H], AF.Exp,
                scale=-0.5), waits=[t_g2a], inc=sC)

            mu_bc = sb_ost[:, 2 * H:3 * H].rearrange(
                "pp (h o) -> pp h o", h=H).broadcast_to([128, H, DH])
            rho_bc = sb_ost[:, 5 * H:6 * H].rearrange(
                "pp (h o) -> pp h o", h=H).broadcast_to([128, H, DH])
            t_g3a = p.op("vector", lambda e: e.tensor_tensor(
                oq3, o3, mu_bc, ALU.subtract),
                waits=[t_gn2] +
                      ([t_obtr_last] if t_obtr_last else []), inc=vC)
            t_g3b = p.op("vector", lambda e: e.tensor_tensor(
                ob3, oq3, rho_bc, ALU.mult), waits=[t_g3a], inc=vC)
            t_g3c = p.op("vector", lambda e: e.tensor_mul(
                sb_osq, sb_ob, sb_lnwbc), waits=[t_g3b], inc=vC)
            t_gn3 = p.op("vector", lambda e: e.tensor_add(
                sb_ob, sb_osq, sb_lnbbc), waits=[t_g3c], inc=vC)
            st["t_gn3"] = t_gn3
            if "obar" in d_dbg:
                p.dma("gpsimd", lambda e, g=ti * CPT + cc: e.dma_start(
                    out=d_dbg["obar"][g], in_=sb_ob), waits=[t_gn3],)

        def emitCobf(cc, st):
            nonlocal t_obtr_last, t_obf_last, t_chunk_tensor_done
            co = cc * C
            t_gn3 = st["t_gn3"]
            t_obf = None
            ttr = None
            for fc in range(FC):
                tr_ps, tr_key = (PF, "PF") if fc % 2 == 0 else (PG, "PG")
                ttr = p.op("tensor", lambda e, fc=fc, ps=tr_ps: e.transpose(
                    ps[:, 0:128], sb_ob[:, fc * 128:(fc + 1) * 128], sb_ident),
                    waits=[t_gn3] + pw(tr_key), inc=tC)
                ow = [ttr]
                if cc == 0 and fc == 0 and t_wo_prev is not None:
                    ow.append(t_wo_prev)
                t_obf = p.op("vector", lambda e, fc=fc, co=co, ps=tr_ps:
                             e.tensor_mul(
                                 sb_obfm[:, fc * TT + co:fc * TT + co + C],
                                 ps[:, 0:128],
                                 sb_r[:, fc * TT + co:fc * TT + co + C]),
                             waits=ow, inc=vC)
                pfree[tr_key] = t_obf
            t_chunk_tensor_done = ttr
            t_obtr_last = ttr
            t_obf_last = t_obf

        ch[0] = emitA(0)
        emitB(0, ch[0])
        ch[1] = emitA(1)
        emitCgn(0, ch[0])
        emitB(1, ch[1])
        emitCobf(0, ch[0])
        emitCgn(1, ch[1])
        emitCobf(1, ch[1])
        t_prep1 = ch[1]["t_prep1"]
        t_su = ch[1]["t_su"]

        t_sev = None
        tmm = None
        for mc in range(FC):
            ps, key = (PA, "PA") if mc % 2 == 0 else (PB, "PB")

            def oproj(e, mc=mc, ps=ps):
                for kc in range(FC):
                    ins = e.matmul(
                        ps[:, 0:TT],
                        sb_W["o"][:, kc * D + mc * 128:kc * D + mc * 128 + 128],
                        sb_obfm[:, kc * TT:(kc + 1) * TT],
                        start=(kc == 0), stop=(kc == FC - 1))
                return ins
            tmm = p.op("tensor", oproj,
                       waits=[t_obf_last, wc_tok["o"]] + pw(key), inc=tC)
            sw = [tmm] + ([t_stile_free] if (t_stile_free and mc == 0) else [])
            t_sev = p.op("scalar", lambda e, mc=mc, ps=ps: e.activation(
                sb_r[:, mc * TT:(mc + 1) * TT], ps[:, 0:TT], AF.Copy),
                waits=sw, inc=sC)
            pfree[key] = t_sev
        t_chunk_tensor_done = tmm
        t_wo_prev = tmm
        t_stile_free = p.dma("gpsimd", lambda e, t0=t0: e.dma_start(
            out=d_s[:, :, t0:t0 + TT],
            in_=sb_r.rearrange("pp (f t) -> pp f t", f=FC)),
            waits=[t_sev],)
        if "sattn" in d_dbg:
            p.dma("gpsimd", lambda e, t0=t0: e.dma_start(
                out=d_dbg["sattn"][:, :, t0:t0 + TT],
                in_=sb_r.rearrange("pp (f t) -> pp f t", f=FC)),
                waits=[t_sev],)

        t_w_free = t_build
        t_k_free = t_kb
        t_r_free = [t_stile_free, t_kb]
        t_v_free = t_su
        t_xsq_free = proj_tok["r"]

    tA = (p.pool_toks('sync') + p.pool_toks('gpsimd') +
          [tC.tok(), vC.tok(), sC.tok(), gC.tok()])
    estA.close()

    if stop_after in ("proj", "attn"):
        p.op("gpsimd", lambda e: e.nop(), waits=p.pool_toks("gpsimd"))
        p.emit()
        return nc

    estB = ExitStack()
    BS = lambda name, shape, dt=F32: estB.enter_context(
        nc.sbuf_tensor(name, shape, dt)).ap()
    sb_g = BS("g_acc", [128, FC * T])
    estB2 = ExitStack()
    B2 = lambda name, shape, dt=F32: estB2.enter_context(
        nc.sbuf_tensor(name, shape, dt)).ap()
    sb_up = [B2(f"up{i}", [128, FC * HID], BF16) for i in range(2)]
    sb_dn = [B2(f"dn{i}", [128, HC * D], BF16) for i in range(2)]
    sb_X = [B2(f"X{i}", [128, FC * ST], BF16) for i in range(2)]
    sb_Xw = [B2(f"Xw{i}", [128, FC * ST], BF16) for i in range(2)]
    sb_Hb = B2("Hb", [128, HC * ST], BF16)
    _srB = B2("srBx", [33, ST])
    sb_srB = [_srB[0:1, :], _srB[32:33, :]]
    sb_sbc = [B2(f"sbcB{i}", [128, ST], BF16) for i in range(2)]

    barB = tA

    def load_cast(dst, dsrc, nchunk, width, extra_waits, guard=None):
        o3 = dst.rearrange("p (c h) -> p c h", c=nchunk)
        i3 = dsrc.rearrange("(c p) h -> p c h", p=128)
        return p.dma("gpsimd", lambda e: e.dma_start(out=o3, in_=i3),
                     waits=list(extra_waits) +
                           ([guard] if guard is not None else []))

    t_supw = load_cast(sb_up[0], d_sup, FC, HID, barB)
    t_sdnw = load_cast(sb_dn[0], d_sdn, HC, D, barB)

    t_X_free = [None, None]
    t_g_tok = {}
    t_H2_rd = None
    t_Xw_free = [None, None]
    t_sbc_free = [None, None]
    t_sr_free = [None, None]

    def stage_subtile(eidx, st_i, weighted):
        nonlocal t_Xw_free, t_sbc_free, t_sr_free
        s0 = st_i * ST
        xslot = st_i % 2
        xw = ([t_X_free[xslot]] if t_X_free[xslot] else []) + barB
        t_x = p.dma("sync", lambda e, s0=s0, xs=xslot: e.dma_start(
            out=sb_X[xs].rearrange("pp (f t) -> pp f t", f=FC),
            in_=d_xnbf[:, :, s0:s0 + ST]), waits=xw)
        if not weighted:
            return (sb_X[xslot], t_x)
        wslot = st_i % 2
        t_sr = p.dma("sync", lambda e, s0=s0, eidx=eidx, ws=wslot: e.dma_start(
            out=sb_srB[ws][0:1, :], in_=d_sel[eidx:eidx + 1, s0:s0 + ST]),
            waits=([t_sr_free[wslot]] if t_sr_free[wslot] else []) + barB)
        t_bcm = p.op("tensor", lambda e, ws=wslot: e.matmul(
            PD[:, 0:ST], sb_ones[32 * ws:32 * ws + 1, 0:128], sb_srB[ws][0:1, :],
            start=True, stop=True), waits=[t_sr] + pw("PD"), inc=tC)
        t_sr_free[wslot] = t_bcm
        t_bce = p.op("vector", lambda e, ws=wslot: e.tensor_copy(
            sb_sbc[ws], PD[:, 0:ST]),
            waits=[t_bcm] + ([t_sbc_free[wslot]] if t_sbc_free[wslot] else []),
            inc=vC)
        pfree["PD"] = t_bce

        def xweight(e, xs=xslot, ws=wslot):
            for kc in range(FC):
                ins = e.tensor_mul(sb_Xw[ws][:, kc * ST:(kc + 1) * ST],
                                   sb_X[xs][:, kc * ST:(kc + 1) * ST],
                                   sb_sbc[ws])
            return ins
        t_xw = p.op("gpsimd", xweight,
                    waits=[t_x, t_bce] +
                          ([t_Xw_free[wslot]] if t_Xw_free[wslot] else []),
                    inc=gC)
        t_sbc_free[wslot] = t_xw
        return (sb_Xw[wslot], t_xw)

    def expert_pass(eidx, up, dn, w_up_tok, w_dn_tok, weighted, prefetch_cb=None,
                    staged0=None):
        nonlocal t_H2_rd
        staged_next = None
        staged = staged0 if staged0 is not None else             stage_subtile(eidx, 0, weighted)
        for st_i in range(NST):
            s0 = st_i * ST
            xslot = st_i % 2
            Xin, xin_tok = staged
            if st_i + 1 < NST:
                staged = stage_subtile(eidx, st_i + 1, weighted)
            elif prefetch_cb is not None:
                staged_next = prefetch_cb()

            t_h2 = None
            t_h2_hist = {}
            t_um = None
            for mc in range(HC):
                ps, key = (PA, "PA") if mc % 2 == 0 else (PB, "PB")

                def up_mm(e, mc=mc, ps=ps, Xin=Xin):
                    for kc in range(FC):
                        ins = e.matmul(
                            ps, up[:, kc * HID + mc * 128:kc * HID + mc * 128 + 128],
                            Xin[:, kc * ST:(kc + 1) * ST],
                            start=(kc == 0), stop=(kc == FC - 1))
                    return ins
                t_um = p.op("tensor", up_mm, waits=[xin_tok, w_up_tok] + pw(key),
                            inc=tC)
                hw = [t_um] + ([t_H2_rd] if (t_H2_rd and mc == 0) else [])
                t_hr = p.op("scalar", lambda e, mc=mc, ps=ps: e.activation(
                    sb_Hb[:, mc * ST:(mc + 1) * ST], ps, AF.Relu), waits=hw, inc=sC)
                pfree[key] = t_hr
                t_h2 = p.op("vector", lambda e, mc=mc: e.tensor_mul(
                    sb_Hb[:, mc * ST:(mc + 1) * ST], sb_Hb[:, mc * ST:(mc + 1) * ST],
                    sb_Hb[:, mc * ST:(mc + 1) * ST]), waits=[t_hr], inc=vC)
                t_h2_hist[mc] = t_h2

            t_dm = None
            for mc2 in range(FC):
                ps, key = (PA, "PA") if mc2 % 2 == 0 else (PB, "PB")

                def dn_mmA(e, mc2=mc2, ps=ps):
                    for kc in range(HC - 1):
                        ins = e.matmul(
                            ps, dn[:, kc * D + mc2 * 128:kc * D + mc2 * 128 + 128],
                            sb_Hb[:, kc * ST:(kc + 1) * ST],
                            start=(kc == 0), stop=False)
                    return ins

                def dn_mmB(e, mc2=mc2, ps=ps):
                    kc = HC - 1
                    return e.matmul(
                        ps, dn[:, kc * D + mc2 * 128:kc * D + mc2 * 128 + 128],
                        sb_Hb[:, kc * ST:(kc + 1) * ST],
                        start=False, stop=True)
                if mc2 == 0:
                    p.op("tensor", dn_mmA,
                         waits=[t_h2_hist[HC - 2], w_dn_tok] + pw(key), inc=tC)
                    t_dm = p.op("tensor", dn_mmB, waits=[t_h2], inc=tC)
                else:
                    def dn_mm(e, mc2=mc2, ps=ps):
                        for kc in range(HC):
                            ins = e.matmul(
                                ps, dn[:, kc * D + mc2 * 128:kc * D + mc2 * 128 + 128],
                                sb_Hb[:, kc * ST:(kc + 1) * ST],
                                start=(kc == 0), stop=(kc == HC - 1))
                        return ins
                    t_dm = p.op("tensor", dn_mm, waits=[t_h2, w_dn_tok] + pw(key),
                                inc=tC)
                gsl0 = mc2 * T + s0
                prior = t_g_tok.get((mc2, st_i))
                if weighted:
                    tok = p.op("vector", lambda e, gsl0=gsl0, ps=ps: e.tensor_add(
                        sb_g[:, gsl0:gsl0 + ST], sb_g[:, gsl0:gsl0 + ST], ps),
                        waits=[t_dm] + ([prior] if prior else []), inc=vC)
                else:
                    tok = p.op("vector", lambda e, gsl0=gsl0, ps=ps: e.tensor_copy(
                        sb_g[:, gsl0:gsl0 + ST], ps), waits=[t_dm], inc=vC)
                t_g_tok[(mc2, st_i)] = tok
                pfree[key] = tok
            t_H2_rd = t_dm
            if weighted:
                t_X_free[xslot] = xin_tok
                t_Xw_free[st_i % 2] = t_um
            else:
                t_X_free[xslot] = t_um
            if st_i == 0 and weighted and eidx + 1 < NE:
                emit_next_weights(eidx + 1)
        return (t_um, t_dm, staged_next if prefetch_cb is not None else None)

    prev = {0: (None, None), 1: (None, None)}
    lc = {}

    def emit_next_weights(enext):
        nsl = (enext + 1) % 2
        gu, gd = prev[nsl]
        lc[enext] = (load_cast(sb_up[nsl], d_wup[enext], FC, HID, [], guard=gu),
                     load_cast(sb_dn[nsl], d_wdn[enext], HC, D, [], guard=gd))

    tokS = expert_pass(-1, sb_up[0], sb_dn[0], t_supw, t_sdnw, weighted=False,
                       prefetch_cb=lambda: stage_subtile(0, 0, True))
    prev[0] = tokS[:2]
    emit_next_weights(0)
    staged_next = tokS[2]
    for eidx in range(NE):
        sl = (eidx + 1) % 2
        nxt = (lambda e2=eidx + 1: stage_subtile(e2, 0, True))             if eidx + 1 < NE else (lambda: None)
        tok = expert_pass(eidx, sb_up[sl], sb_dn[sl], lc[eidx][0], lc[eidx][1],
                          weighted=True, prefetch_cb=nxt, staged0=staged_next)
        prev[sl] = tok[:2]
        staged_next = tok[2]

    if "g" in d_dbg:
        for st_i in range(NST):
            for mc2 in range(FC):
                p.op("gpsimd", lambda e, mc2=mc2, st_i=st_i: e.dma_start(
                    out=d_dbg["g"][:, mc2, st_i * ST:(st_i + 1) * ST],
                    in_=sb_g[:, mc2 * T + st_i * ST:mc2 * T + st_i * ST + ST]),
                    waits=[t_g_tok[(mc2, st_i)]],)

    tB = [tC.tok(), vC.tok(), sC.tok(), gC.tok()] + p.pool_toks('sync')
    estB2.close()

    if stop_after == "moe":
        p.op("gpsimd", lambda e: e.nop(), waits=p.pool_toks("gpsimd"))
        p.emit()
        return nc

    estC = ExitStack()
    CS = lambda name, shape, dt=F32: estC.enter_context(
        nc.sbuf_tensor(name, shape, dt)).ap()
    sb_xc = [CS(f"xc{i}", [128, FC * ST]) for i in range(2)]
    sb_sc2 = [CS(f"sc2{i}", [128, FC * ST]) for i in range(2)]
    sb_gs = CS("gs", [128, FC * ST])
    sb_oc = [CS(f"oc{i}", [128, FC * ST]) for i in range(2)]

    t_oc_free = [None, None]
    t_cb_prev = None
    for ct in range(NST):
        s0 = ct * ST
        slot = ct % 2
        t_xl = p.dma("sync", lambda e, s0=s0, sl=slot: e.dma_start(
            out=sb_xc[sl].rearrange("pp (f t) -> pp f t", f=FC),
            in_=x3[:, :, s0:s0 + ST]),
            waits=barB + tB + ([t_oc_free[slot]] if t_oc_free[slot] else []))
        t_sl = p.dma("sync", lambda e, s0=s0, sl=slot: e.dma_start(
            out=sb_sc2[sl].rearrange("pp (f t) -> pp f t", f=FC),
            in_=d_s[:, :, s0:s0 + ST]), waits=tB)

        def silu(e, s0=s0):
            for fcm in range(FC):
                ins = e.activation(sb_gs[:, fcm * ST:(fcm + 1) * ST],
                                   sb_g[:, fcm * T + s0:fcm * T + s0 + ST], AF.Silu)
            return ins
        t_si = p.op("scalar", silu,
                    waits=tB + ([t_cb_prev] if t_cb_prev else []), inc=sC)

        def comb1(e, sl=slot):
            for fcm in range(FC):
                ins = e.tensor_mul(sb_oc[sl][:, fcm * ST:(fcm + 1) * ST],
                                   sb_sc2[sl][:, fcm * ST:(fcm + 1) * ST],
                                   sb_gs[:, fcm * ST:(fcm + 1) * ST])
            return ins

        def comb2(e, sl=slot):
            for fcm in range(FC):
                ins = e.tensor_add(sb_oc[sl][:, fcm * ST:(fcm + 1) * ST],
                                   sb_oc[sl][:, fcm * ST:(fcm + 1) * ST],
                                   sb_xc[sl][:, fcm * ST:(fcm + 1) * ST])
            return ins
        t_cb0 = p.op("vector", comb1, waits=[t_xl, t_sl, t_si], inc=vC)
        t_cb = p.op("vector", comb2, waits=[t_cb0], inc=vC)
        t_cb_prev = t_cb
        t_oc_free[slot] = p.dma("gpsimd", lambda e, s0=s0, sl=slot: e.dma_start(
            out=d_out.rearrange("(f pp) t -> pp f t", pp=128)[:, :, s0:s0 + ST],
            in_=sb_oc[sl].rearrange("pp (f t) -> pp f t", f=FC)),
            waits=[t_cb])

    estC.close()
    estB.close()
    p.op("gpsimd", lambda e: e.nop(), waits=p.pool_toks("gpsimd"))
    p.emit()
    return nc


def make_in_map(inputs, core):
    cn = np.ascontiguousarray
    x = np.asarray(inputs["L0_attn_x"])
    m = {
        "x_t": cn(x[core].T),
        "wrT": cn(np.asarray(inputs["L0_attn_W_r"]).T),
        "wkT": cn(np.asarray(inputs["L0_attn_W_k"]).T),
        "wvT": cn(np.asarray(inputs["L0_attn_W_v"]).T),
        "wwT": cn(np.asarray(inputs["L0_attn_W_w"]).T),
        "woT": cn(np.asarray(inputs["L0_attn_W_o"]).T),
        "u_pp": cn(np.asarray(inputs["L0_attn_u"]).reshape(D).reshape(FC, 128).T),
        "tmr_pp": cn(np.asarray(inputs["L0_attn_tm_r"]).reshape(FC, 128).T),
        "tmk_pp": cn(np.asarray(inputs["L0_attn_tm_k"]).reshape(FC, 128).T),
        "tmv_pp": cn(np.asarray(inputs["L0_attn_tm_v"]).reshape(FC, 128).T),
        "tmw_pp": cn(np.asarray(inputs["L0_attn_tm_w"]).reshape(FC, 128).T),
        "lnw_row": cn(np.tile(np.asarray(inputs["L0_attn_ln_w"]), H)),
        "lnb_row": cn(np.tile(np.asarray(inputs["L0_attn_ln_b"]), H)),
        "gate_wT": cn(np.asarray(inputs["L0_ffn_gate_w"]).T),
        "ebias": cn(np.asarray(inputs["L0_ffn_expert_bias"])),
        "w_upT": cn(np.asarray(inputs["L0_ffn_w_up"]).transpose(0, 2, 1)),
        "w_downT": cn(np.asarray(inputs["L0_ffn_w_down"]).transpose(0, 2, 1)),
        "sh_upT": cn(np.asarray(inputs["L0_ffn_shared_w_up"]).T),
        "sh_downT": cn(np.asarray(inputs["L0_ffn_shared_w_down"]).T),
    }
    return {k: np.asarray(v, dtype=np.float32) for k, v in m.items()}


def kernel(**inputs):
    nc = build()
    in_maps = [make_in_map(inputs, c) for c in range(8)]
    res = run_bass_kernel_spmd(nc, in_maps, core_ids=list(range(8)))
    outs = [np.ascontiguousarray(res.results[c]["out_t"].T) for c in range(8)]
    return np.stack(outs, axis=0)


# version 25 (final)
# speedup vs baseline: 1.1982x; 1.1982x over previous
import numpy as np
import ml_dtypes
from contextlib import ExitStack

import concourse.bass as bass
import concourse.mybir as mybir
from concourse.bass_utils import run_bass_kernel_spmd

F32 = mybir.dt.float32
BF16 = mybir.dt.bfloat16
AF = mybir.ActivationFunctionType
ALU = mybir.AluOpType

T, D, H, DH, FC = 2048, 1024, 16, 64, 8
TT = 256
NT = T // TT
C = 128
CPT = TT // C
HID, HC, NE = 1408, 11, 8
ST = 512
NST = T // ST
EPS = 1e-5


class Sem:
    __slots__ = ("h", "c", "name")

    def __init__(self, h, name):
        self.h, self.c, self.name = h, 0, name

    def tok(self):
        return (self, self.c)


class Prog:
    def __init__(self, nc):
        self.nc = nc
        self.ops = []
        self.pools = {}

    def sem(self, name):
        return Sem(self.nc.alloc_semaphore(name), name)

    def make_pool(self, name, eng, n):
        self.pools[eng] = [self.sem(f"{name}{i}") for i in range(n)]
        self.pools[eng + "_idx"] = 0

    def dma(self, eng, fn, waits=()):
        pool = self.pools[eng]
        i = self.pools[eng + "_idx"]
        self.pools[eng + "_idx"] = (i + 1) % len(pool)
        sem = pool[i]
        w = list(waits) + ([sem.tok()] if sem.c > 0 else [])
        return self.op(eng, fn, waits=w, inc=sem, inc_n=16)

    def pool_toks(self, eng):
        return [sm.tok() for sm in self.pools[eng]]

    def op(self, eng, fn, waits=(), inc=None, inc_n=1):
        incs = []
        if inc is not None:
            inc.c += inc_n
            incs.append((inc, inc_n))
        self.ops.append((eng, fn,
                         [(s, v) for (s, v) in waits if s is not None and v > 0],
                         incs))
        return inc.tok() if inc is not None else None

    def emit(self):
        def mk(engname):
            def body(eng):
                waited = {}
                for (e, fn, waits, incs) in self.ops:
                    if e != engname:
                        continue
                    for (s, v) in waits:
                        if waited.get(id(s), 0) < v:
                            eng.wait_ge(s.h, v)
                            waited[id(s)] = v
                    ins = fn(eng)
                    for (s, n) in incs:
                        ins = ins.then_inc(s.h, n)
            return body

        with self.nc.Block() as block:
            block.sync(mk("sync"))
            block.tensor(mk("tensor"))
            block.vector(mk("vector"))
            block.scalar(mk("scalar"))
            block.gpsimd(mk("gpsimd"))


def build(dbg=(), stop_after="full"):
    nc = bass.Bass("TRN2", target_bir_lowering=False)
    p = Prog(nc)
    dbg = set(dbg)

    d_x = nc.dram_tensor("x_t", [D, T], F32, kind="ExternalInput").ap()
    d_W = {m: nc.dram_tensor(f"w{m}T", [D, D], F32, kind="ExternalInput").ap()
           for m in "rkvwo"}
    d_u = nc.dram_tensor("u_pp", [128, FC], F32, kind="ExternalInput").ap()
    d_tm = {m: nc.dram_tensor(f"tm{m}_pp", [128, FC], F32, kind="ExternalInput").ap()
            for m in "rkvw"}
    d_lnw = nc.dram_tensor("lnw_row", [D], F32, kind="ExternalInput").ap()
    d_lnb = nc.dram_tensor("lnb_row", [D], F32, kind="ExternalInput").ap()
    d_gw = nc.dram_tensor("gate_wT", [D, NE], F32, kind="ExternalInput").ap()
    d_eb = nc.dram_tensor("ebias", [NE], F32, kind="ExternalInput").ap()
    d_wup = nc.dram_tensor("w_upT", [NE, D, HID], F32, kind="ExternalInput").ap()
    d_wdn = nc.dram_tensor("w_downT", [NE, HID, D], F32, kind="ExternalInput").ap()
    d_sup = nc.dram_tensor("sh_upT", [D, HID], F32, kind="ExternalInput").ap()
    d_sdn = nc.dram_tensor("sh_downT", [HID, D], F32, kind="ExternalInput").ap()
    d_out = nc.dram_tensor("out_t", [D, T], F32, kind="ExternalOutput").ap()

    d_xnbf = nc.dram_tensor("xnbf_scr", [128, FC, T], BF16, kind="Internal").ap()
    d_s = nc.dram_tensor("s_scr", [128, FC, T], F32, kind="Internal").ap()
    d_sel = nc.dram_tensor("sel_scr", [NE, T], F32, kind="Internal").ap()

    d_dbg = {}
    dbg_shapes = {"xn": [128, FC, T], "scores": [NE, T], "sel": [NE, T],
                  "w": [128, FC, T], "k": [128, FC, T], "r": [128, FC, T],
                  "vtm": [T // C, 128, D], "otm": [T // C, 128, D],
                  "obar": [T // C, 128, D], "sattn": [128, FC, T],
                  "g": [128, FC, T], "zw": [128, FC, T],
                  "btm": [NT * CPT, 128, NE], "stm": [NT * CPT, 128, NE],
                  "sel8": [NT * CPT, 128, NE], "gt3": [NT * CPT, 128, NE * NE],
                  "eq3": [NT * CPT, 128, NE * NE], "rkA": [NT * CPT, 128, NE],
                  "rkB": [NT * CPT, 128, NE], "rkT": [NT * CPT, 128, NE]}
    for name in dbg:
        d_dbg[name] = nc.dram_tensor("dbg_" + name, dbg_shapes[name], F32,
                                     kind="ExternalOutput").ap()

    c_ident = nc.inline_tensor(np.eye(128, dtype=np.float32), "identc")
    _b2 = np.zeros((128, 2), dtype=ml_dtypes.bfloat16)
    _b2[0:64, 0] = 1
    _b2[64:128, 1] = 1
    c_ones2 = nc.inline_tensor(_b2, "ones2c")
    CSH = 0.8
    _t = np.arange(128, dtype=np.float64)
    _msk = np.triu(np.ones((128, 128)), 1) * np.exp(
        -CSH * (_t[None, :] - _t[:, None] - 1.0))
    c_mask = nc.inline_tensor(_msk.astype(np.float32), "maskc")
    c_rampinv = nc.inline_tensor(
        np.broadcast_to(np.exp(-CSH * (_t - 63.0)), (128, 128)).astype(np.float32),
        "rampinvc")
    c_mask4 = nc.inline_tensor(
        np.tile(_msk.astype(np.float32), (1, 4)), "mask4c")
    c_lcol = nc.inline_tensor(
        np.exp(CSH * (_t - 63.0)).astype(np.float32)[:, None], "lcolc")

    p.make_pool("dL", "sync", 6)
    p.make_pool("dS", "gpsimd", 4)
    tC = p.sem("tC")
    vC = p.sem("vC")
    sC = p.sem("sC")
    gC = p.sem("gC")

    A = lambda name, shape, dt=F32: nc.alloc_sbuf_tensor(name, shape, dt).ap()
    sb_ones = A("ones_f", [128, 128])
    sb_ones_bf = A("ones_bf", [128, 2], BF16)
    sb_ident = A("idents", [128, 128])
    sb_mask = A("masks", [128, 128])
    sb_lcol = A("lcol", [128, 1])
    sb_nc63 = A("nc63", [128, 1])
    sb_u = A("u_sb", [128, FC])
    sb_tm = {m: A(f"tm_{m}", [128, FC]) for m in "rkvw"}
    sb_gw = A("gw_sb", [128, FC * NE])
    sb_eb = A("eb_sb", [NE, 1])

    PA = nc.alloc_psum_tensor("PA", [128, 512], F32).ap()
    PB = nc.alloc_psum_tensor("PB", [128, 512], F32).ap()
    PCfull = nc.alloc_psum_tensor("PC", [128, 1024], F32).ap()
    PC0 = PCfull[:, 0:512]
    PC1 = PCfull[:, 512:1024]
    PD = nc.alloc_psum_tensor("PD", [128, 512], F32).ap()
    PEb = nc.alloc_psum_tensor("PEb", [128, 512], F32).ap()
    PF = nc.alloc_psum_tensor("PF", [128, 512], F32).ap()
    PG = nc.alloc_psum_tensor("PG", [128, 512], F32).ap()

    pfree = {}
    dbg_guard = {}

    def pw(*keys):
        return [pfree[k] for k in keys if k in pfree]

    p.op("vector", lambda e: e.memset(sb_ones, 1.0))
    const_toks = []
    const_toks.append(p.dma("sync", lambda e: e.dma_start(
        out=sb_ones_bf, in_=c_ones2.ap())))
    p.op("vector", lambda e: e.memset(sb_nc63, -63.0 * 0.8), inc=vC)
    for (dstap, srct) in ((sb_ident, c_ident), (sb_mask, c_mask),
                          (sb_lcol, c_lcol)):
        const_toks.append(p.dma("sync", lambda e, d=dstap, s=srct: e.dma_start(
            out=d, in_=s.ap())))
    const_toks.append(p.dma("sync", lambda e: e.dma_start(out=sb_u, in_=d_u)))
    for m in "rkvw":
        const_toks.append(p.dma("sync", lambda e, m=m: e.dma_start(
            out=sb_tm[m], in_=d_tm[m])))
    const_toks.append(p.dma("sync", lambda e: e.dma_start(
        out=sb_gw.rearrange("p (f e) -> p f e", f=FC),
        in_=d_gw.rearrange("(f p) e -> p f e", p=128))))
    const_toks.append(p.dma("sync", lambda e: e.dma_start(
        out=sb_eb[:, 0:1], in_=d_eb[:, None])))
    t_ones = p.op("vector", lambda e: e.engine_nop(), waits=const_toks, inc=vC)
    t_const = t_ones

    estA = ExitStack()
    AS = lambda name, shape, dt=F32: estA.enter_context(
        nc.sbuf_tensor(name, shape, dt)).ap()
    sb_W = {m: AS(f"W{m}_bf", [128, FC * D], BF16) for m in "rkvwo"}
    sb_wstage = AS("wstage", [128, 1024])
    sb_lnwbc = AS("lnwbc", [128, D])
    sb_lnbbc = AS("lnbbc", [128, D])
    sb_lnrow = AS("lnrow", [1, D])
    sb_xt = AS("xt", [128, FC * TT])
    sb_xsq = AS("xsq", [128, FC * TT])
    sb_xn = AS("xn_ext", [128, FC * (TT + 1)])
    sb_row = AS("rows", [1, 4 * TT])
    sb_delta = sb_xt
    sb_tmd = AS("tmd2", [128, 2 * TT])
    sb_xmix = [AS(f"xmix{i}", [128, FC * TT], BF16) for i in range(2)]
    sb_r = AS("r_f32", [128, FC * TT])
    sb_k = AS("k_f32", [128, FC * TT])
    sb_w = AS("w_f32", [128, FC * TT])
    sb_wtmp = AS("wtmp", [128, TT])
    sb_v = AS("v_tm", [128, CPT * D], BF16)
    sb_V = AS("Vcum", [128, FC * C])
    sb_excl = AS("excl", [128, FC * C])
    sb_E = AS("Ebuf", [128, 8 * C])
    sb_mcol = AS("mcol", [128, 3 * FC])
    sb_decay = AS("decay", [128, 2 * FC])
    sb_Rp = AS("Rp_f32", [128, C])
    sb_Rpbf = AS("Rp_bf", [128, FC * C], BF16)
    sb_Rbf = AS("R_bf", [128, FC * C], BF16)
    sb_Kpad = AS("Kpad", [128, H * C], BF16)
    sb_Khat = AS("Khat", [128, C])
    sb_Khtm = AS("Khtm", [128, FC * C], BF16)
    sb_B = AS("B_bf", [128, C], BF16)
    sb_Am = AS("Amask", [128, 4 * C], BF16)
    sb_d = AS("d_tm", [128, H])
    sb_otm = AS("o_tm", [128, D])
    sb_osq = sb_xsq[:, 0:D]
    sb_ost = AS("ostats", [128, 6 * H])
    sb_ob = sb_xsq[:, D:2 * D]
    sb_obfm = AS("obar_fm", [128, FC * TT], BF16)
    sb_S = AS("S_f32", [128, H * DH])
    sb_Sbf = AS("S_bf", [128, H * DH], BF16)
    sb_sc = AS("scA", [NE, TT])
    sb_bia = AS("scB", [NE, TT])
    sb_btm = AS("b_tm", [128, NE])
    sb_stm = AS("s_tm", [128, NE])
    sb_cmp = AS("cmp8", [128, NE])
    sb_gt3 = AS("gt3", [128, NE * NE])
    sb_eq3 = AS("eq3", [128, NE * NE])
    sb_rkA = AS("rkA", [128, NE])
    sb_rkB = AS("rkB", [128, NE])
    sb_sel8 = AS("sel8", [128, NE])
    sb_rank = AS("rank", [128, 4])
    sb_srow = AS("srow", [NE, TT])

    p.op("vector", lambda e: e.memset(sb_rank[:, 3:4], 2.0))
    p.op("vector", lambda e: e.memset(sb_eq3, 0.0), inc=vC)
    p.op("vector", lambda e: e.memset(sb_S, 0.0))
    p.op("vector", lambda e: e.memset(sb_Sbf, 0.0))
    p.op("vector", lambda e: e.memset(sb_Kpad, 0.0))
    t_carry0 = p.op("vector", lambda e: e.memset(sb_xn, 0.0), inc=vC)

    wc_tok = {}
    for m in "rkvwo":
        wc_tok[m] = p.dma("gpsimd", lambda e, m=m: e.dma_start(
            out=sb_W[m].rearrange("p (c d) -> p c d", c=8),
            in_=d_W[m].rearrange("(c p) d -> p c d", p=128)))
    t_stage_free = wc_tok["o"]
    sb_Rp2 = sb_wstage[:, 0:C]
    sb_rampinv = sb_wstage[:, 128:256]
    sb_mask4 = sb_wstage[:, 384:896]
    t_rampld = p.dma("sync", lambda e: e.dma_start(
        out=sb_rampinv, in_=c_rampinv.ap()), waits=[t_stage_free])
    t_rampld2 = p.dma("sync", lambda e: e.dma_start(
        out=sb_mask4, in_=c_mask4.ap()), waits=[t_stage_free])

    lt = p.dma("sync", lambda e: e.dma_start(out=sb_lnrow[0:1, 0:D], in_=d_lnw[None, :]))
    b1 = p.op("tensor", lambda e: e.matmul(PC0, sb_ones[0:1, 0:128],
                                           sb_lnrow[0:1, 0:512], start=True, stop=True),
              waits=[lt, t_ones], inc=tC)
    b2 = p.op("tensor", lambda e: e.matmul(PC1, sb_ones[0:1, 0:128],
                                           sb_lnrow[0:1, 512:1024], start=True, stop=True),
              waits=[lt], inc=tC)
    c1 = p.op("vector", lambda e: e.tensor_copy(sb_lnwbc, PCfull),
              waits=[b1, b2], inc=vC)
    lt2 = p.dma("sync", lambda e: e.dma_start(out=sb_lnrow[0:1, 0:D], in_=d_lnb[None, :]),
               waits=[c1])
    b3 = p.op("tensor", lambda e: e.matmul(PC0, sb_ones[0:1, 0:128],
                                           sb_lnrow[0:1, 0:512], start=True, stop=True),
              waits=[lt2, c1], inc=tC)
    b4 = p.op("tensor", lambda e: e.matmul(PC1, sb_ones[0:1, 0:128],
                                           sb_lnrow[0:1, 512:1024],
                                           start=True, stop=True),
              waits=[lt2], inc=tC)
    t_lnb = p.op("vector", lambda e: e.tensor_copy(sb_lnbbc, PCfull),
                 waits=[b3, b4], inc=vC)
    pfree["PC0"] = t_lnb
    pfree["PC1"] = t_lnb

    t_srw_tail = None
    t_build = None
    t_kb = None
    t_ktr_tr = None
    t_dmm = None
    prev_kb_hist = None
    t_su0_prev = None
    t_obf_last = None
    t_xt_free = None
    t_xnbf_free = None
    t_srow_free = None
    t_stile_free = None
    t_buf_free = {0: None, 1: None}
    t_chunk_tensor_done = None
    t_headmm_done = None
    t_obtr_last = None
    t_wo_prev = None
    t_w_free = None
    t_xsq_free = None
    t_k_free = None
    t_r_free = None
    t_v_free = None

    x3 = d_x.rearrange("(f pp) t -> pp f t", pp=128)
    xt3 = sb_xt.rearrange("pp (f t) -> pp f t", f=FC)

    for ti in range(NT):
        t0 = ti * TT

        t_xload = p.dma("sync", lambda e, t0=t0: e.dma_start(
            out=xt3, in_=x3[:, :, t0:t0 + TT]),
            waits=[t_xt_free] if t_xt_free else [])

        if ti > 0:
            def carry(e):
                for fc in range(FC):
                    base = fc * (TT + 1)
                    ins = e.tensor_copy(sb_xn[:, base:base + 1],
                                        sb_xn[:, base + TT:base + TT + 1])
                return ins
            t_carry = p.op("vector", carry, inc=vC)
        else:
            t_carry = t_carry0

        sqw = [t_xload]
        if t_obtr_last is not None:
            sqw.append(t_obtr_last)
        if t_xsq_free is not None:
            sqw.append(t_xsq_free)
        t_sq = p.op("vector", lambda e: e.tensor_mul(sb_xsq, sb_xt, sb_xt),
                    waits=sqw, inc=vC)

        def stats(e, off, srcap):
            for fc in range(FC):
                ins = e.matmul(PD[0:1, off:off + TT], sb_ones[:, 0:1],
                               srcap[:, fc * TT:(fc + 1) * TT],
                               start=(fc == 0), stop=(fc == FC - 1))
            return ins
        t_st1 = p.op("tensor", lambda e: stats(e, 0, sb_xt),
                     waits=[t_xload, t_ones] + pw("PD"), inc=tC)
        t_st2 = p.op("tensor", lambda e: stats(e, TT, sb_xsq), waits=[t_sq], inc=tC)

        t_mu = p.op("vector", lambda e: e.tensor_scalar(
            sb_row[0:1, 0:TT], PD[0:1, 0:TT], 1.0 / D, None, ALU.mult),
            waits=[t_st2], inc=vC)
        t_e2 = p.op("vector", lambda e: e.tensor_scalar(
            sb_row[0:1, TT:2 * TT], PD[0:1, TT:2 * TT], 1.0 / D, None, ALU.mult),
            inc=vC)
        pfree["PD"] = t_e2
        t_m2 = p.op("vector", lambda e: e.tensor_mul(
            sb_row[0:1, 2 * TT:3 * TT], sb_row[0:1, 0:TT], sb_row[0:1, 0:TT]),
            waits=[t_e2], inc=vC)
        t_m3 = p.op("vector", lambda e: e.tensor_sub(
            sb_row[0:1, 2 * TT:3 * TT], sb_row[0:1, TT:2 * TT],
            sb_row[0:1, 2 * TT:3 * TT]), waits=[t_m2], inc=vC)
        t_var = p.op("vector", lambda e: e.tensor_scalar(
            sb_row[0:1, 2 * TT:3 * TT], sb_row[0:1, 2 * TT:3 * TT],
            EPS, None, ALU.add), waits=[t_m3], inc=vC)
        t_ln1 = p.op("scalar", lambda e: e.activation(sb_row[0:1, 2 * TT:3 * TT],
                                                      sb_row[0:1, 2 * TT:3 * TT],
                                                      AF.Ln), waits=[t_var], inc=sC)
        t_rho = p.op("scalar", lambda e: e.activation(sb_row[0:1, TT:2 * TT],
                                                      sb_row[0:1, 2 * TT:3 * TT],
                                                      AF.Exp, scale=-0.5),
                     waits=[t_ln1], inc=sC)
        t_mr = p.op("vector", lambda e: e.tensor_mul(sb_row[0:1, 3 * TT:4 * TT],
                                                     sb_row[0:1, 0:TT],
                                                     sb_row[0:1, TT:2 * TT]),
                    waits=[t_rho, t_mu], inc=vC)
        t_bc1 = p.op("tensor", lambda e: e.matmul(PA[:, 0:TT], sb_ones[0:1, 0:128],
                                                  sb_row[0:1, TT:2 * TT],
                                                  start=True, stop=True),
                     waits=[t_mr] + pw("PA"), inc=tC)
        t_bc2 = p.op("tensor", lambda e: e.matmul(PB[:, 0:TT], sb_ones[0:1, 0:128],
                                                  sb_row[0:1, 3 * TT:4 * TT],
                                                  start=True, stop=True),
                     waits=[t_mr] + pw("PB"), inc=tC)

        def norm0(e):
            for fc in range(FC):
                ins = e.tensor_mul(sb_xsq[:, fc * TT:(fc + 1) * TT],
                                   sb_xt[:, fc * TT:(fc + 1) * TT], PA[:, 0:TT])
            return ins

        def norm(e):
            for fc in range(FC):
                base = fc * (TT + 1)
                ins = e.tensor_sub(sb_xn[:, base + 1:base + 1 + TT],
                                   sb_xsq[:, fc * TT:(fc + 1) * TT], PB[:, 0:TT])
            return ins
        t_xn0 = p.op("vector", norm0,
                     waits=[t_bc1, t_bc2, t_carry] +
                           ([t_xnbf_free] if t_xnbf_free else []), inc=vC)
        t_xn = p.op("vector", norm, waits=[t_xn0], inc=vC)
        pfree["PA"] = t_xn
        pfree["PB"] = t_xn
        t_xt_free = t_xn

        t_xnbf_free = p.dma("gpsimd", lambda e, t0=t0: e.dma_start(
            out=d_xnbf[:, :, t0:t0 + TT],
            in_=sb_xn.rearrange("pp (f t) -> pp f t", f=FC, t=TT + 1)[:, :, 1:TT + 1]),
            waits=[t_xn],)
        if "xn" in d_dbg:
            p.dma("gpsimd", lambda e, t0=t0: e.dma_start(
                out=d_dbg["xn"][:, :, t0:t0 + TT],
                in_=sb_xn.rearrange("pp (f t) -> pp f t", f=FC, t=TT + 1)[:, :, 1:TT + 1]),
                waits=[t_xn],)

        def gmm(e):
            for fc in range(FC):
                base = fc * (TT + 1)
                ins = e.matmul(PD[0:NE, 0:TT], sb_gw[:, fc * NE:(fc + 1) * NE],
                               sb_xn[:, base + 1:base + 1 + TT],
                               start=(fc == 0), stop=(fc == FC - 1))
            return ins
        t_g = p.op("tensor", gmm, waits=[t_xn, t_const] + pw("PD"), inc=tC)
        t_ge = p.op("scalar", lambda e: e.activation(sb_bia[0:NE, :], PD[0:NE, 0:TT],
                                                     AF.Exp, scale=-1.0),
                    waits=[t_g], inc=sC)
        t_pd_done = p.op("scalar", lambda e: e.activation(
            sb_sc[0:NE, :], sb_bia[0:NE, :], AF.Ln, bias=1.0),
            waits=[t_ge], inc=sC)
        pfree["PD"] = t_pd_done
        t_sc1 = p.op("scalar", lambda e: e.activation(
            sb_bia[0:NE, :], sb_sc[0:NE, :], AF.Exp, scale=-1.0),
            waits=[t_pd_done], inc=sC)
        t_bia = p.op("vector", lambda e: e.tensor_scalar(
            sb_sc[0:NE, :], sb_bia[0:NE, :], sb_eb[0:NE, 0:1], None, ALU.add),
            waits=[t_sc1], inc=vC)
        if "scores" in d_dbg:
            p.dma("gpsimd", lambda e, t0=t0: e.dma_start(
                out=d_dbg["scores"][:, t0:t0 + TT], in_=sb_bia[0:NE, :]),
                waits=[t_bia],)

        t_srw = None
        for cc in range(CPT):
            t_srw_prev = t_srw if cc > 0 else t_srw_tail
            co = cc * C
            tb = p.op("tensor", lambda e, co=co: e.transpose(
                PG[:, 0:NE], sb_sc[0:NE, co:co + C], sb_ident[0:NE, 0:NE]),
                waits=[t_bia] + pw("PG"), inc=tC)
            tb2 = p.op("tensor", lambda e, co=co: e.transpose(
                PG[:, 8:8 + NE], sb_bia[0:NE, co:co + C], sb_ident[0:NE, 0:NE]),
                inc=tC)
            tv = p.op("vector", lambda e: e.tensor_copy(sb_btm, PG[:, 0:NE]),
                      waits=[tb, tb2] + ([t_srw_prev] if t_srw_prev else []) +
                            ([dbg_guard["btm"]] if "btm" in dbg_guard else []),
                      inc=vC)
            tv2 = p.op("vector", lambda e: e.tensor_copy(sb_stm, PG[:, 8:8 + NE]),
                       inc=vC)
            pfree["PG"] = tv2

            def select_w(e):
                for j in range(NE):
                    e.tensor_scalar(sb_gt3[:, j * NE:(j + 1) * NE], sb_btm,
                                    sb_btm[:, j:j + 1], None, ALU.is_gt)
                for j in range(1, NE):
                    ins = e.tensor_scalar(sb_eq3[:, j * NE:j * NE + j],
                                          sb_btm[:, 0:j], sb_btm[:, j:j + 1], None,
                                          ALU.is_equal)
                return ins

            def select(e):
                e.tensor_reduce(sb_rkA.rearrange("p (a o) -> p a o", o=1),
                                sb_gt3.rearrange("p (a b) -> p a b", a=NE),
                                mybir.AxisListType.X, ALU.add)
                ins = e.tensor_reduce(sb_rkB.rearrange("p (a o) -> p a o", o=1),
                                      sb_eq3.rearrange("p (a b) -> p a b", a=NE),
                                      mybir.AxisListType.X, ALU.add)
                return ins
            if "btm" in d_dbg:
                p.dma("gpsimd", lambda e, ti=ti, cc=cc: e.dma_start(
                    out=d_dbg["btm"][ti * CPT + cc], in_=sb_btm),
                    waits=[tv],)
                dbg_guard["btm"] = p.dma("gpsimd", lambda e, ti=ti, cc=cc: e.dma_start(
                    out=d_dbg["stm"][ti * CPT + cc], in_=sb_stm),
                    waits=[tv2],)
            selw = [tv, tv2]
            for nm in ("gt3", "sel8"):
                if nm in dbg_guard:
                    selw.append(dbg_guard[nm])
            t_selw = p.op("vector", select_w, waits=selw, inc=vC)
            t_sel0 = p.op("vector", select, waits=[t_selw], inc=vC)

            t_sA = p.op("vector", lambda e: e.tensor_add(sb_cmp, sb_rkA, sb_rkB),
                        waits=[t_sel0], inc=vC)
            t_sB = p.op("vector", lambda e: e.tensor_scalar(
                sb_sel8, sb_cmp, sb_ones[:, 0:1], None, ALU.is_le),
                waits=[t_sA], inc=vC)
            t_sel = p.op("vector", lambda e: e.tensor_mul(sb_sel8, sb_sel8, sb_stm),
                        waits=[t_sB], inc=vC)
            if "gt3" in d_dbg:
                for nm, src in (("gt3", sb_gt3), ("eq3", sb_eq3), ("rkA", sb_rkA),
                                ("rkB", sb_rkB), ("rkT", sb_cmp)):
                    dbg_guard["gt3"] = p.dma(
                        "gpsimd", lambda e, ti=ti, cc=cc, nm=nm, src=src:
                        e.dma_start(out=d_dbg[nm][ti * CPT + cc], in_=src),
                        waits=[t_sel])
            if "sel8" in d_dbg:
                dbg_guard["sel8"] = p.dma("gpsimd", lambda e, ti=ti, cc=cc: e.dma_start(
                    out=d_dbg["sel8"][ti * CPT + cc], in_=sb_sel8),
                    waits=[t_sel],)
            tbk = p.op("tensor", lambda e: e.transpose(
                PG[0:NE, 0:128], sb_sel8, sb_ident),
                waits=[t_sel] + pw("PG"), inc=tC)
            sw = [tbk]
            if t_srow_free is not None and cc == 0:
                sw.append(t_srow_free)
            t_srw = p.op("vector", lambda e, co=co: e.tensor_copy(
                sb_srow[0:NE, co:co + C], PG[0:NE, 0:128]), waits=sw, inc=vC)
            pfree["PG"] = t_srw
        t_srw_tail = t_srw
        t_srow_free = p.dma("gpsimd", lambda e, t0=t0: e.dma_start(
            out=d_sel[:, t0:t0 + TT], in_=sb_srow[0:NE, :]),
            waits=[t_srw],)
        if "sel" in d_dbg:
            p.dma("gpsimd", lambda e, t0=t0: e.dma_start(
                out=d_dbg["sel"][:, t0:t0 + TT], in_=sb_srow[0:NE, :]),
                waits=[t_srw],)

        proj_tok = {}
        wvts = {}
        t_wtmp_rd = None

        def mkdelta(e):
            for fc in range(FC):
                base = fc * (TT + 1)
                ins = e.tensor_sub(sb_delta[:, fc * TT:(fc + 1) * TT],
                                   sb_xn[:, base + 1:base + 1 + TT],
                                   sb_xn[:, base:base + TT])
            return ins
        t_delta = p.op("vector", mkdelta, waits=[t_xn], inc=vC)
        tmd_tok = {}
        for mi, m in enumerate("wkvr"):
            buf = sb_xmix[mi % 2]
            t_mix = None
            for fc in range(FC):
                slot = (mi * FC + fc) % 2
                tsw = [t_delta]
                if tmd_tok.get(slot - 2 if slot >= 2 else (mi * FC + fc) - 2):
                    pass
                prev = tmd_tok.get((mi * FC + fc) - 2)
                ts_t = p.op("scalar", lambda e, m=m, fc=fc, slot=slot: e.activation(
                    sb_tmd[:, slot * TT:(slot + 1) * TT],
                    sb_delta[:, fc * TT:(fc + 1) * TT], AF.Copy,
                    scale=sb_tm[m][:, fc:fc + 1]),
                    waits=[t_delta] + ([prev] if prev else []), inc=sC)
                base = fc * (TT + 1)
                t_mix = p.op("vector", lambda e, fc=fc, slot=slot, buf=buf,
                             base=base: e.tensor_add(
                                 buf[:, fc * TT:(fc + 1) * TT],
                                 sb_tmd[:, slot * TT:(slot + 1) * TT],
                                 sb_xn[:, base:base + TT]),
                             waits=[ts_t] +
                                   ([t_buf_free[mi % 2]] if (fc == 0 and
                                    t_buf_free[mi % 2] is not None) else []),
                             inc=vC)
                tmd_tok[mi * FC + fc] = t_mix
            t_xt_free = t_mix

            if m == "v":
                vro = [t_v_free] if t_v_free else []
                for cc in range(CPT):
                    def vproj(e, buf=buf, cc=cc):
                        for half, ps in ((0, PC0), (1, PC1)):
                            for kc in range(FC):
                                ins = e.matmul(
                                    ps,
                                    buf[:, kc * TT + cc * C:kc * TT + cc * C + C],
                                    sb_W["v"][:, kc * D + half * 512:
                                              kc * D + half * 512 + 512],
                                    start=(kc == 0), stop=(kc == FC - 1))
                        return ins
                    tmm = p.op("tensor", vproj,
                               waits=[t_mix, wc_tok["v"]] + pw("PC0", "PC1"),
                               inc=tC)
                    p.op("scalar", lambda e, cc=cc: e.activation(
                        sb_v[:, cc * D:cc * D + 512], PC0, AF.Copy),
                        waits=[tmm] + (vro if cc == 0 else []), inc=sC)
                    tev = p.op("scalar", lambda e, cc=cc: e.activation(
                        sb_v[:, cc * D + 512:(cc + 1) * D], PC1, AF.Copy),
                        inc=sC)
                    pfree["PC0"] = tev
                    pfree["PC1"] = tev
                    proj_tok[("v", cc)] = tev
                    if "vtm" in d_dbg:
                        p.dma("gpsimd", lambda e, cc=cc, ti=ti: e.dma_start(
                            out=d_dbg["vtm"][ti * CPT + cc],
                            in_=sb_v[:, cc * D:(cc + 1) * D]),
                            waits=[tev],)
                t_buf_free[mi % 2] = tmm
                continue

            last_ev = None
            for mc in range(FC):
                ps, key = (PA, "PA") if mc % 2 == 0 else (PB, "PB")

                def fproj(e, buf=buf, m=m, mc=mc, ps=ps):
                    for kc in range(FC):
                        ins = e.matmul(
                            ps[:, 0:TT],
                            sb_W[m][:, kc * D + mc * 128:kc * D + mc * 128 + 128],
                            buf[:, kc * TT:(kc + 1) * TT],
                            start=(kc == 0), stop=(kc == FC - 1))
                    return ins
                tmm = p.op("tensor", fproj,
                           waits=[t_mix, wc_tok[m]] + pw(key), inc=tC)
                sl0, sl1 = mc * TT, (mc + 1) * TT
                if m == "k":
                    kw = [tmm] + ([t_k_free] if (t_k_free and mc == 0) else [])
                    last_ev = p.op("vector", lambda e, sl0=sl0, sl1=sl1, ps=ps:
                                   e.tensor_copy(sb_k[:, sl0:sl1], ps[:, 0:TT]),
                                   waits=kw, inc=vC)
                elif m == "w":
                    if "zw" in d_dbg:
                        zsl = (mc % 4) * TT
                        tzc = p.op("vector", lambda e, ps=ps, zsl=zsl: e.tensor_copy(
                            sb_excl[:, zsl:zsl + TT], ps[:, 0:TT]),
                            waits=[tmm], inc=vC)
                        p.dma("gpsimd", lambda e, mc=mc, t0=t0, zsl=zsl: e.dma_start(
                            out=d_dbg["zw"][:, mc, t0:t0 + TT],
                            in_=sb_excl[:, zsl:zsl + TT]), waits=[tzc],)
                        tmm = tzc
                    ww = [tmm] + ([t_w_free] if (t_w_free and mc == 0) else [])
                    if wvts.get(mc - 2) is not None:
                        ww.append(wvts[mc - 2])
                    vslot = sb_V[:, (mc % 2) * TT:(mc % 2) * TT + TT]
                    if t_wtmp_rd is not None:
                        ww.append(t_wtmp_rd)
                    t_we = p.op("scalar", lambda e, ps=ps: e.activation(
                        sb_wtmp, ps[:, 0:TT], AF.Exp, scale=-1.0), waits=ww,
                        inc=sC)
                    t_wln = p.op("scalar", lambda e, vs=vslot: e.activation(
                        vs, sb_wtmp, AF.Ln, bias=1.0), waits=[t_we], inc=sC)
                    t_wtmp_rd = t_wln
                    last_ev = p.op("vector", lambda e, sl0=sl0, sl1=sl1, vs=vslot:
                                   e.tensor_scalar(sb_w[:, sl0:sl1], vs,
                                                   -1.0, -1e-4, ALU.mult, ALU.add),
                                   waits=[t_wln], inc=vC)
                    wvts[mc] = last_ev
                else:
                    rw = [tmm] + (t_r_free if (t_r_free and mc == 0) else [])
                    if t_wtmp_rd is not None:
                        rw.append(t_wtmp_rd)
                    t_re = p.op("scalar", lambda e, ps=ps: e.activation(
                        sb_wtmp, ps[:, 0:TT], AF.Exp, scale=-1.0), waits=rw,
                        inc=sC)
                    t_rl = p.op("scalar", lambda e, sl0=sl0, sl1=sl1: e.activation(
                        sb_xsq[:, sl0:sl1], sb_wtmp, AF.Ln, bias=1.0),
                        waits=[t_re], inc=sC)
                    t_wtmp_rd = t_rl
                    last_ev = p.op("scalar", lambda e, sl0=sl0, sl1=sl1: e.activation(
                        sb_r[:, sl0:sl1], sb_xsq[:, sl0:sl1], AF.Exp, scale=-1.0),
                        waits=[t_rl], inc=sC)
                pfree[key] = last_ev
            proj_tok[m] = last_ev
            t_buf_free[mi % 2] = tmm

        for nm in ("w", "k", "r"):
            if nm in d_dbg:
                srcmap = {"w": sb_w, "k": sb_k, "r": sb_r}
                p.dma("gpsimd", lambda e, nm=nm, t0=t0: e.dma_start(
                    out=d_dbg[nm][:, :, t0:t0 + TT],
                    in_=srcmap[nm].rearrange("pp (f t) -> pp f t", f=FC)),
                    waits=[proj_tok[nm]],)

        if stop_after == "proj":
            t_w_free = proj_tok["w"]
            t_k_free = proj_tok["k"]
            t_r_free = [proj_tok["r"]]
            t_v_free = proj_tok[("v", CPT - 1)]
            t_xsq_free = proj_tok["r"]
            continue

        ch = {}

        def emitA(cc):
            nonlocal t_build, t_kb, t_ktr_tr, t_dmm, t_headmm_done, prev_kb_hist
            st = {}
            co = cc * C

            def prep1a0(e, co=co):
                for fc in range(FC):
                    wsl = slice(fc * TT + co, fc * TT + co + C)
                    ins = e.tensor_scalar(sb_excl[:, fc * C:(fc + 1) * C],
                                          sb_w[:, wsl], 0.8, None, ALU.add)
                return ins

            def prep1a(e):
                for fc in range(FC):
                    ins = e.tensor_tensor_scan(sb_V[:, fc * C:(fc + 1) * C],
                                               sb_ones[:, 0:C],
                                               sb_excl[:, fc * C:(fc + 1) * C],
                                               0.0, ALU.mult, ALU.add)
                return ins

            def prep1(e):
                for fc in range(FC):
                    e.tensor_sub(sb_excl[:, fc * C:(fc + 1) * C],
                                 sb_V[:, fc * C:(fc + 1) * C],
                                 sb_excl[:, fc * C:(fc + 1) * C])
                    e.tensor_scalar(sb_mcol[:, fc:fc + 1],
                                    sb_V[:, fc * C + 63:fc * C + 64],
                                    -1.0, None, ALU.mult)
                    e.tensor_scalar(sb_mcol[:, FC + fc:FC + fc + 1],
                                    sb_V[:, fc * C + 127:fc * C + 128],
                                    -64.0 * 0.8, None, ALU.add)
                    ins = e.tensor_scalar(sb_mcol[:, 2 * FC + fc:2 * FC + fc + 1],
                                          sb_V[:, fc * C + 127:fc * C + 128],
                                          -128.0 * 0.8, None, ALU.add)
                return ins
            prw = [proj_tok["w"]]
            if ti == 0 and cc == 0:
                prw.append(t_rampld)
                prw.append(t_rampld2)
            if t_kb is not None:
                prw.append(t_kb)
            t_prep1a0 = p.op("vector", prep1a0, waits=prw, inc=vC)
            t_prep1a = p.op("vector", prep1a, waits=[t_prep1a0], inc=vC)
            t_prep1 = p.op("vector", prep1, waits=[t_prep1a], inc=vC)
            st["t_prep1"] = t_prep1

            def cols(e):
                for fc in range(FC):
                    e.activation(sb_decay[:, fc:fc + 1],
                                 sb_mcol[:, 2 * FC + fc:2 * FC + fc + 1],
                                 AF.Exp)
                    ins = e.activation(sb_decay[:, FC + fc:FC + fc + 1],
                                       sb_mcol[:, 2 * FC + fc:2 * FC + fc + 1],
                                       AF.Exp)
                return ins
            t_cols = p.op("scalar", cols,
                          waits=[t_prep1] +
                                ([t_su0_prev] if t_su0_prev else []), inc=sC)

            t_build_hist = {}
            t_kb_hist = {}
            t_ktr_ev = None
            for fc in range(FC):
                eo = (fc % 2) * 4 * C

                def eops(e, fc=fc, co=co, eo=eo):
                    e.activation(sb_E[:, eo:eo + C],
                                 sb_excl[:, fc * C:(fc + 1) * C], AF.Exp,
                                 bias=sb_mcol[:, fc:fc + 1])
                    e.activation(sb_E[:, eo + C:eo + 2 * C],
                                 sb_V[:, fc * C:(fc + 1) * C],
                                 AF.Exp, scale=-1.0,
                                 bias=sb_V[:, fc * C + 63:fc * C + 64])
                    e.activation(sb_E[:, eo + 2 * C:eo + 3 * C],
                                 sb_V[:, fc * C:(fc + 1) * C],
                                 AF.Exp, scale=-1.0,
                                 bias=sb_mcol[:, FC + fc:FC + fc + 1])
                    e.activation(sb_w[:, fc * TT + co:fc * TT + co + C],
                                 sb_excl[:, fc * C:(fc + 1) * C], AF.Exp,
                                 bias=sb_nc63[:, 0:1])
                    ins = e.activation(sb_E[:, eo + 3 * C:eo + 4 * C],
                                       sb_k[:, fc * TT + co:fc * TT + co + C],
                                       AF.Exp, bias=sb_u[:, fc:fc + 1])
                    return ins
                ew = [t_prep1]
                if fc == 0:
                    ew.append(proj_tok["k"])
                if t_build_hist.get(fc - 2):
                    ew.append(t_build_hist[fc - 2])
                if t_kb_hist.get(fc - 2):
                    ew.append(t_kb_hist[fc - 2])
                if fc < 2 and prev_kb_hist is not None:
                    ew.append(prev_kb_hist[fc + 6])
                t_eops = p.op("scalar", eops, waits=ew, inc=sC)

                def buildf0(e, fc=fc, co=co, eo=eo):
                    ksl = slice(fc * TT + co, fc * TT + co + C)
                    e.tensor_mul(sb_Rp2, sb_r[:, ksl], sb_w[:, ksl])
                    return e.tensor_mul(sb_Rp, sb_r[:, ksl], sb_E[:, eo:eo + C])

                def buildf(e, fc=fc):
                    e.tensor_mul(sb_Rpbf[:, fc * C:(fc + 1) * C], sb_Rp2,
                                 sb_rampinv)
                    if fc < 2:
                        h0, h1 = 2 * fc, 2 * fc + 1
                        e.memset(sb_Kpad[64:128, h0 * C:(h0 + 1) * C], 0.0)
                        e.memset(sb_Kpad[0:64, h1 * C:(h1 + 1) * C], 0.0)
                    ins = e.tensor_copy(sb_Rbf[:, fc * C:(fc + 1) * C], sb_Rp)
                    return ins

                def buildk(e, fc=fc, co=co, eo=eo):
                    h0, h1 = 2 * fc, 2 * fc + 1
                    ksl = slice(fc * TT + co, fc * TT + co + C)
                    e.tensor_tensor(sb_Kpad[0:64, h0 * C:(h0 + 1) * C],
                                    sb_k[0:64, ksl], sb_E[0:64, eo + C:eo + 2 * C],
                                    ALU.mult)
                    e.tensor_tensor(sb_Kpad[64:128, h1 * C:(h1 + 1) * C],
                                    sb_k[64:128, ksl],
                                    sb_E[64:128, eo + C:eo + 2 * C], ALU.mult)
                    e.tensor_mul(sb_Khat, sb_k[:, ksl],
                                 sb_E[:, eo + 2 * C:eo + 3 * C])
                    ins = e.tensor_mul(sb_B, sb_r[:, ksl],
                                       sb_E[:, eo + 3 * C:eo + 4 * C])
                    return ins
                b0w = [t_eops, t_cols]
                if fc == 0:
                    b0w.append(proj_tok["r"])
                if t_build is not None:
                    b0w.append(t_build)
                if fc == 0 and t_headmm_done is not None:
                    b0w.append(t_headmm_done)
                t_build0 = p.op("vector", buildf0, waits=b0w, inc=vC)
                t_build = p.op("vector", buildf, waits=[t_build0], inc=vC)
                t_build_hist[fc] = t_build
                kbw = [t_eops, t_cols]
                if fc == 0:
                    kbw.append(proj_tok["r"])
                if t_ktr_tr is not None:
                    kbw.append(t_ktr_tr)
                if t_dmm is not None:
                    kbw.append(t_dmm)
                if fc == 0 and t_headmm_done is not None:
                    kbw.append(t_headmm_done)
                t_kb = p.op("gpsimd", buildk, waits=kbw, inc=gC)
                t_kb_hist[fc] = t_kb

                ktr_ps = PEb if fc % 2 == 0 else PF
                ktr_key = "PEb" if fc % 2 == 0 else "PF"
                tt1 = p.op("tensor", lambda e, ps=ktr_ps: e.transpose(
                    ps[:, 0:128], sb_Khat, sb_ident),
                    waits=[t_kb] + pw(ktr_key), inc=tC)
                t_ktr_tr = tt1
                t_ktr_ev = p.op("vector", lambda e, fc=fc, ps=ktr_ps: e.tensor_scalar(
                    sb_Khtm[:, fc * C:(fc + 1) * C], ps[:, 0:128],
                    sb_lcol[:, 0:1], None, ALU.mult),
                    waits=[tt1], inc=vC)
                pfree[ktr_key] = t_ktr_ev
                t_dmm = p.op("tensor", lambda e, fc=fc: e.matmul(
                    PD[:, 128 + 2 * fc:128 + 2 * fc + 2], sb_B, sb_ones_bf,
                    start=True, stop=True),
                    waits=[t_kb] + (pw("PD") if fc == 0 else []), inc=tC)
            t_dev = p.op("vector", lambda e: e.tensor_copy(sb_d, PD[:, 128:128 + H]),
                         waits=[t_dmm], inc=vC)
            pfree["PD"] = t_dev
            st["t_build_hist"] = t_build_hist
            st["t_kb_hist"] = t_kb_hist
            st["t_ktr_ev"] = t_ktr_ev
            st["t_dev"] = t_dev
            prev_kb_hist = t_kb_hist
            return st

        def emitB(cc, st):
            nonlocal t_headmm_done, t_su0_prev, t_chunk_tensor_done
            scol = lambda h: (h % 2) * 512 + (h // 2) * 64
            t_build_hist = st["t_build_hist"]
            t_kb_hist = st["t_kb_hist"]
            t_ktr_ev = st["t_ktr_ev"]
            t_dev = st["t_dev"]
            last_tm2 = {}
            tam_hist = {}
            tm1_hist = {}
            tm3 = None
            tm4 = None

            def emit_tm1(g):
                a_ps, a_key = (PA, "PA") if g % 2 == 0 else (PB, "PB")
                tm1 = None
                for hi in range(4):
                    h = g * 4 + hi
                    fc = h // 2
                    tm1 = p.op("tensor", lambda e, h=h, fc=fc, hi=hi, ps=a_ps:
                               e.matmul(ps[:, hi * C:(hi + 1) * C],
                                        sb_Kpad[:, h * C:(h + 1) * C],
                                        sb_Rbf[:, fc * C:(fc + 1) * C],
                                        start=True, stop=True),
                               waits=([t_build_hist[2 * g + 1],
                                       t_kb_hist[2 * g + 1]] + pw(a_key))
                                     if hi == 0 else [],
                               inc=tC)
                tm1_hist[g] = tm1

            def emit_grp(g):
                nonlocal tm3, tm4, t_chunk_tensor_done
                a_ps, a_key = (PA, "PA") if g % 2 == 0 else (PB, "PB")
                amb = sb_Am if g % 2 == 0 else sb_Kpad[:, 0:4 * C]
                tam = p.op("vector", lambda e, ps=a_ps, amb=amb: e.tensor_mul(
                    amb, ps[:, 0:512], sb_mask4),
                    waits=[tm1_hist[g]] +
                          ([last_tm2[g - 2]] if g - 2 in last_tm2 else []),
                    inc=vC)
                pfree[a_key] = tam
                tam_hist[g] = tam
                for hi in range(4):
                    h = g * 4 + hi
                    fc = h // 2
                    tm2 = p.op("tensor", lambda e, h=h, hi=hi, cc=cc, amb=amb:
                               e.matmul(
                        PCfull[:, h * DH:(h + 1) * DH],
                        amb[:, hi * C:(hi + 1) * C],
                        sb_v[:, cc * D + h * DH:cc * D + (h + 1) * DH],
                        start=True, stop=False),
                        waits=([tam] + (pw("PC0", "PC1") if h == 0 else []) +
                               ([proj_tok[("v", cc)]] if h == 0 else [])),
                        inc=tC)
                    tm3 = p.op("tensor", lambda e, h=h, fc=fc, scol=scol: e.matmul(
                        PCfull[:, h * DH:(h + 1) * DH],
                        sb_Rpbf[:, fc * C:(fc + 1) * C],
                        sb_Sbf[:, scol(h):scol(h) + DH],
                        start=False, stop=True), inc=tC)
                    d_ps, d_key = (PF, "PF") if h % 2 == 0 else (PG, "PG")
                    tm4 = p.op("tensor", lambda e, h=h, fc=fc, cc=cc, ps=d_ps:
                               e.matmul(ps[:, fc * DH:(fc + 1) * DH],
                                        sb_Khtm[:, fc * C:(fc + 1) * C],
                                        sb_v[:, cc * D + h * DH:cc * D + (h + 1) * DH],
                                        start=True, stop=True),
                               waits=([t_ktr_ev] + pw(d_key)) if h < 2 else [],
                               inc=tC)
                    last_tm2[g] = tm2

            emit_tm1(0)
            emit_tm1(1)
            emit_grp(0)
            emit_tm1(2)
            emit_grp(1)
            emit_tm1(3)
            emit_grp(2)
            emit_grp(3)
            t_chunk_tensor_done = tm4

            d_bc = sb_d.rearrange("pp (h o) -> pp h o", h=H).broadcast_to(
                [128, H, DH])
            v3c = sb_v[:, cc * D:(cc + 1) * D].rearrange(
                "pp (h j) -> pp h j", h=H)
            otm3v = sb_otm.rearrange("pp (h j) -> pp h j", h=H)
            t_oe0 = p.op("vector", lambda e, v3c=v3c: e.tensor_tensor(
                otm3v, v3c, d_bc, ALU.mult),
                waits=[t_dev] + ([dbg_guard["otm"]] if "otm" in dbg_guard else []),
                inc=vC)
            t_oev = p.op("vector", lambda e: e.tensor_add(
                sb_otm, sb_otm, PCfull[:, 0:D]), waits=[tm3, t_oe0], inc=vC)
            pfree["PC0"] = t_oev
            pfree["PC1"] = t_oev

            dec3 = sb_decay.rearrange("pp (f o) -> pp f o", f=2 * FC).broadcast_to(
                [128, 2 * FC, DH])
            S3 = sb_S.rearrange("pp (f j) -> pp f j", f=2 * FC)
            t_su0 = p.op("vector", lambda e: e.tensor_tensor(
                S3, S3, dec3, ALU.mult), waits=[tm4], inc=vC)
            t_su1a = p.op("vector", lambda e: e.tensor_add(
                sb_S[0:64, 0:512], sb_S[0:64, 0:512], PF[0:64, 0:512]),
                waits=[t_su0], inc=vC)
            pfree["PF"] = t_su1a
            t_su1b = p.op("vector", lambda e: e.tensor_add(
                sb_S[64:128, 512:1024], sb_S[64:128, 512:1024], PG[64:128, 0:512]),
                waits=[t_su1a], inc=vC)
            pfree["PG"] = t_su1b
            t_su = p.op("vector", lambda e: e.tensor_copy(sb_Sbf, sb_S),
                        waits=[t_su1b], inc=vC)
            t_headmm_done = tm4
            t_su0_prev = t_su0
            st["t_oev"] = t_oev
            st["t_su"] = t_su
            if "otm" in d_dbg:
                dbg_guard["otm"] = p.dma("gpsimd", lambda e, g=ti * CPT + cc:
                                         e.dma_start(out=d_dbg["otm"][g],
                                                     in_=sb_otm),
                                         waits=[t_oev],)

        def emitCgn(cc, st):
            t_oev = st["t_oev"]
            o3 = sb_otm.rearrange("pp (h j) -> pp h j", h=H)
            ob3 = sb_ob.rearrange("pp (h j) -> pp h j", h=H)
            oq3 = sb_osq.rearrange("pp (h j) -> pp h j", h=H)

            t_gsq = p.op("vector", lambda e: e.tensor_mul(sb_osq, sb_otm, sb_otm),
                         waits=[t_oev], inc=vC)

            def gn1(e):
                e.tensor_reduce(sb_ost[:, 0:H].rearrange("pp (h o) -> pp h o", h=H),
                                o3, mybir.AxisListType.X, ALU.add)
                return e.tensor_reduce(
                    sb_ost[:, H:2 * H].rearrange("pp (h o) -> pp h o", h=H),
                    oq3, mybir.AxisListType.X, ALU.add)
            t_gn1a = p.op("vector", gn1, waits=[t_gsq], inc=vC)

            t_gb1 = p.op("vector", lambda e: (
                e.tensor_scalar(sb_ost[:, 2 * H:3 * H], sb_ost[:, 0:H], 1.0 / DH,
                                None, ALU.mult),
                e.tensor_scalar(sb_ost[:, 3 * H:4 * H], sb_ost[:, H:2 * H],
                                1.0 / DH, None, ALU.mult))[-1],
                waits=[t_gn1a], inc=vC)
            t_gb2 = p.op("vector", lambda e: e.tensor_mul(
                sb_ost[:, 4 * H:5 * H], sb_ost[:, 2 * H:3 * H],
                sb_ost[:, 2 * H:3 * H]), waits=[t_gb1], inc=vC)
            t_gn1 = p.op("vector", lambda e: e.tensor_sub(
                sb_ost[:, 4 * H:5 * H], sb_ost[:, 3 * H:4 * H],
                sb_ost[:, 4 * H:5 * H]), waits=[t_gb2], inc=vC)

            t_gn1b = p.op("vector", lambda e: e.tensor_scalar(
                sb_ost[:, 4 * H:5 * H], sb_ost[:, 4 * H:5 * H], EPS, None,
                ALU.add), waits=[t_gn1], inc=vC)

            t_g2a = p.op("scalar", lambda e: e.activation(
                sb_ost[:, 4 * H:5 * H], sb_ost[:, 4 * H:5 * H], AF.Ln),
                waits=[t_gn1b], inc=sC)
            t_gn2 = p.op("scalar", lambda e: e.activation(
                sb_ost[:, 5 * H:6 * H], sb_ost[:, 4 * H:5 * H], AF.Exp,
                scale=-0.5), waits=[t_g2a], inc=sC)

            mu_bc = sb_ost[:, 2 * H:3 * H].rearrange(
                "pp (h o) -> pp h o", h=H).broadcast_to([128, H, DH])
            rho_bc = sb_ost[:, 5 * H:6 * H].rearrange(
                "pp (h o) -> pp h o", h=H).broadcast_to([128, H, DH])
            t_g3a = p.op("vector", lambda e: e.tensor_tensor(
                oq3, o3, mu_bc, ALU.subtract),
                waits=[t_gn2] +
                      ([t_obtr_last] if t_obtr_last else []), inc=vC)
            t_g3b = p.op("vector", lambda e: e.tensor_tensor(
                ob3, oq3, rho_bc, ALU.mult), waits=[t_g3a], inc=vC)
            t_g3c = p.op("vector", lambda e: e.tensor_mul(
                sb_osq, sb_ob, sb_lnwbc), waits=[t_g3b], inc=vC)
            t_gn3 = p.op("vector", lambda e: e.tensor_add(
                sb_ob, sb_osq, sb_lnbbc), waits=[t_g3c], inc=vC)
            st["t_gn3"] = t_gn3
            if "obar" in d_dbg:
                p.dma("gpsimd", lambda e, g=ti * CPT + cc: e.dma_start(
                    out=d_dbg["obar"][g], in_=sb_ob), waits=[t_gn3],)

        def emitCobf(cc, st):
            nonlocal t_obtr_last, t_obf_last, t_chunk_tensor_done
            co = cc * C
            t_gn3 = st["t_gn3"]
            t_obf = None
            ttr = None
            for fc in range(FC):
                tr_ps, tr_key = (PF, "PF") if fc % 2 == 0 else (PG, "PG")
                ttr = p.op("tensor", lambda e, fc=fc, ps=tr_ps: e.transpose(
                    ps[:, 0:128], sb_ob[:, fc * 128:(fc + 1) * 128], sb_ident),
                    waits=[t_gn3] + pw(tr_key), inc=tC)
                ow = [ttr]
                if cc == 0 and fc == 0 and t_wo_prev is not None:
                    ow.append(t_wo_prev)
                t_obf = p.op("vector", lambda e, fc=fc, co=co, ps=tr_ps:
                             e.tensor_mul(
                                 sb_obfm[:, fc * TT + co:fc * TT + co + C],
                                 ps[:, 0:128],
                                 sb_r[:, fc * TT + co:fc * TT + co + C]),
                             waits=ow, inc=vC)
                pfree[tr_key] = t_obf
            t_chunk_tensor_done = ttr
            t_obtr_last = ttr
            t_obf_last = t_obf

        ch[0] = emitA(0)
        emitB(0, ch[0])
        ch[1] = emitA(1)
        emitCgn(0, ch[0])
        emitB(1, ch[1])
        emitCobf(0, ch[0])
        emitCgn(1, ch[1])
        emitCobf(1, ch[1])
        t_prep1 = ch[1]["t_prep1"]
        t_su = ch[1]["t_su"]

        t_sev = None
        tmm = None
        for mc in range(FC):
            ps, key = (PA, "PA") if mc % 2 == 0 else (PB, "PB")

            def oproj(e, mc=mc, ps=ps):
                for kc in range(FC):
                    ins = e.matmul(
                        ps[:, 0:TT],
                        sb_W["o"][:, kc * D + mc * 128:kc * D + mc * 128 + 128],
                        sb_obfm[:, kc * TT:(kc + 1) * TT],
                        start=(kc == 0), stop=(kc == FC - 1))
                return ins
            tmm = p.op("tensor", oproj,
                       waits=[t_obf_last, wc_tok["o"]] + pw(key), inc=tC)
            sw = [tmm] + ([t_stile_free] if (t_stile_free and mc == 0) else [])
            t_sev = p.op("scalar", lambda e, mc=mc, ps=ps: e.activation(
                sb_r[:, mc * TT:(mc + 1) * TT], ps[:, 0:TT], AF.Copy),
                waits=sw, inc=sC)
            pfree[key] = t_sev
        t_chunk_tensor_done = tmm
        t_wo_prev = tmm
        t_stile_free = p.dma("gpsimd", lambda e, t0=t0: e.dma_start(
            out=d_s[:, :, t0:t0 + TT],
            in_=sb_r.rearrange("pp (f t) -> pp f t", f=FC)),
            waits=[t_sev],)
        if "sattn" in d_dbg:
            p.dma("gpsimd", lambda e, t0=t0: e.dma_start(
                out=d_dbg["sattn"][:, :, t0:t0 + TT],
                in_=sb_r.rearrange("pp (f t) -> pp f t", f=FC)),
                waits=[t_sev],)

        t_w_free = t_build
        t_k_free = t_kb
        t_r_free = [t_stile_free, t_kb]
        t_v_free = t_su
        t_xsq_free = proj_tok["r"]

    tA = (p.pool_toks('sync') + p.pool_toks('gpsimd') +
          [tC.tok(), vC.tok(), sC.tok(), gC.tok()])
    estA.close()

    if stop_after in ("proj", "attn"):
        p.op("gpsimd", lambda e: e.nop(), waits=p.pool_toks("gpsimd"))
        p.emit()
        return nc

    estB = ExitStack()
    BS = lambda name, shape, dt=F32: estB.enter_context(
        nc.sbuf_tensor(name, shape, dt)).ap()
    sb_g = BS("g_acc", [128, FC * T])
    estB2 = ExitStack()
    B2 = lambda name, shape, dt=F32: estB2.enter_context(
        nc.sbuf_tensor(name, shape, dt)).ap()
    sb_up = [B2(f"up{i}", [128, FC * HID], BF16) for i in range(2)]
    sb_dn = [B2(f"dn{i}", [128, HC * D], BF16) for i in range(2)]
    sb_X = [B2(f"X{i}", [128, FC * ST], BF16) for i in range(2)]
    sb_Xw = [B2(f"Xw{i}", [128, FC * ST], BF16) for i in range(2)]
    sb_Hb = B2("Hb", [128, HC * ST], BF16)
    _srB = B2("srBx", [33, ST])
    sb_srB = [_srB[0:1, :], _srB[32:33, :]]
    sb_sbc = [B2(f"sbcB{i}", [128, ST], BF16) for i in range(2)]

    barB = tA

    def load_cast(dst, dsrc, nchunk, width, extra_waits, guard=None):
        o3 = dst.rearrange("p (c h) -> p c h", c=nchunk)
        i3 = dsrc.rearrange("(c p) h -> p c h", p=128)
        return p.dma("gpsimd", lambda e: e.dma_start(out=o3, in_=i3),
                     waits=list(extra_waits) +
                           ([guard] if guard is not None else []))

    t_supw = load_cast(sb_up[0], d_sup, FC, HID, barB)
    t_sdnw = load_cast(sb_dn[0], d_sdn, HC, D, barB)

    t_X_free = [None, None]
    t_g_tok = {}
    t_H2_rd = None
    t_Xw_free = [None, None]
    t_sbc_free = [None, None]
    t_sr_free = [None, None]

    def stage_subtile(eidx, st_i, weighted):
        nonlocal t_Xw_free, t_sbc_free, t_sr_free
        s0 = st_i * ST
        xslot = st_i % 2
        xw = ([t_X_free[xslot]] if t_X_free[xslot] else []) + barB
        t_x = p.dma("sync", lambda e, s0=s0, xs=xslot: e.dma_start(
            out=sb_X[xs].rearrange("pp (f t) -> pp f t", f=FC),
            in_=d_xnbf[:, :, s0:s0 + ST]), waits=xw)
        if not weighted:
            return (sb_X[xslot], t_x)
        wslot = st_i % 2
        t_sr = p.dma("sync", lambda e, s0=s0, eidx=eidx, ws=wslot: e.dma_start(
            out=sb_srB[ws][0:1, :], in_=d_sel[eidx:eidx + 1, s0:s0 + ST]),
            waits=([t_sr_free[wslot]] if t_sr_free[wslot] else []) + barB)
        t_bcm = p.op("tensor", lambda e, ws=wslot: e.matmul(
            PD[:, 0:ST], sb_ones[32 * ws:32 * ws + 1, 0:128], sb_srB[ws][0:1, :],
            start=True, stop=True), waits=[t_sr] + pw("PD"), inc=tC)
        t_sr_free[wslot] = t_bcm
        t_bce = p.op("vector", lambda e, ws=wslot: e.tensor_copy(
            sb_sbc[ws], PD[:, 0:ST]),
            waits=[t_bcm] + ([t_sbc_free[wslot]] if t_sbc_free[wslot] else []),
            inc=vC)
        pfree["PD"] = t_bce

        def xweight(e, xs=xslot, ws=wslot):
            for kc in range(FC):
                ins = e.tensor_mul(sb_Xw[ws][:, kc * ST:(kc + 1) * ST],
                                   sb_X[xs][:, kc * ST:(kc + 1) * ST],
                                   sb_sbc[ws])
            return ins
        t_xw = p.op("gpsimd", xweight,
                    waits=[t_x, t_bce] +
                          ([t_Xw_free[wslot]] if t_Xw_free[wslot] else []),
                    inc=gC)
        t_sbc_free[wslot] = t_xw
        return (sb_Xw[wslot], t_xw)

    def expert_pass(eidx, up, dn, w_up_tok, w_dn_tok, weighted, prefetch_cb=None,
                    staged0=None):
        nonlocal t_H2_rd
        staged_next = None
        staged = staged0 if staged0 is not None else             stage_subtile(eidx, 0, weighted)
        for st_i in range(NST):
            s0 = st_i * ST
            xslot = st_i % 2
            Xin, xin_tok = staged
            if st_i + 1 < NST:
                staged = stage_subtile(eidx, st_i + 1, weighted)
            elif prefetch_cb is not None:
                staged_next = prefetch_cb()

            t_h2 = None
            t_h2_hist = {}
            t_um = None
            for mc in range(HC):
                ps, key = (PA, "PA") if mc % 2 == 0 else (PB, "PB")

                def up_mm(e, mc=mc, ps=ps, Xin=Xin):
                    for kc in range(FC):
                        ins = e.matmul(
                            ps, up[:, kc * HID + mc * 128:kc * HID + mc * 128 + 128],
                            Xin[:, kc * ST:(kc + 1) * ST],
                            start=(kc == 0), stop=(kc == FC - 1))
                    return ins
                t_um = p.op("tensor", up_mm, waits=[xin_tok, w_up_tok] + pw(key),
                            inc=tC)
                hw = [t_um] + ([t_H2_rd] if (t_H2_rd and mc == 0) else [])
                t_hr = p.op("scalar", lambda e, mc=mc, ps=ps: e.activation(
                    sb_Hb[:, mc * ST:(mc + 1) * ST], ps, AF.Relu), waits=hw, inc=sC)
                pfree[key] = t_hr
                t_h2 = p.op("vector", lambda e, mc=mc: e.tensor_mul(
                    sb_Hb[:, mc * ST:(mc + 1) * ST], sb_Hb[:, mc * ST:(mc + 1) * ST],
                    sb_Hb[:, mc * ST:(mc + 1) * ST]), waits=[t_hr], inc=vC)
                t_h2_hist[mc] = t_h2

            t_dm = None
            for mc2 in range(FC):
                ps, key = (PA, "PA") if mc2 % 2 == 0 else (PB, "PB")

                def dn_mmA(e, mc2=mc2, ps=ps):
                    for kc in range(HC - 1):
                        ins = e.matmul(
                            ps, dn[:, kc * D + mc2 * 128:kc * D + mc2 * 128 + 128],
                            sb_Hb[:, kc * ST:(kc + 1) * ST],
                            start=(kc == 0), stop=False)
                    return ins

                def dn_mmB(e, mc2=mc2, ps=ps):
                    kc = HC - 1
                    return e.matmul(
                        ps, dn[:, kc * D + mc2 * 128:kc * D + mc2 * 128 + 128],
                        sb_Hb[:, kc * ST:(kc + 1) * ST],
                        start=False, stop=True)
                if mc2 == 0:
                    p.op("tensor", dn_mmA,
                         waits=[t_h2_hist[HC - 2], w_dn_tok] + pw(key), inc=tC)
                    t_dm = p.op("tensor", dn_mmB, waits=[t_h2], inc=tC)
                else:
                    def dn_mm(e, mc2=mc2, ps=ps):
                        for kc in range(HC):
                            ins = e.matmul(
                                ps, dn[:, kc * D + mc2 * 128:kc * D + mc2 * 128 + 128],
                                sb_Hb[:, kc * ST:(kc + 1) * ST],
                                start=(kc == 0), stop=(kc == HC - 1))
                        return ins
                    t_dm = p.op("tensor", dn_mm, waits=[t_h2, w_dn_tok] + pw(key),
                                inc=tC)
                gsl0 = mc2 * T + s0
                prior = t_g_tok.get((mc2, st_i))
                if weighted:
                    tok = p.op("vector", lambda e, gsl0=gsl0, ps=ps: e.tensor_add(
                        sb_g[:, gsl0:gsl0 + ST], sb_g[:, gsl0:gsl0 + ST], ps),
                        waits=[t_dm] + ([prior] if prior else []), inc=vC)
                else:
                    tok = p.op("vector", lambda e, gsl0=gsl0, ps=ps: e.tensor_copy(
                        sb_g[:, gsl0:gsl0 + ST], ps), waits=[t_dm], inc=vC)
                t_g_tok[(mc2, st_i)] = tok
                pfree[key] = tok
            t_H2_rd = t_dm
            if weighted:
                t_X_free[xslot] = xin_tok
                t_Xw_free[st_i % 2] = t_um
            else:
                t_X_free[xslot] = t_um
            if st_i == 0 and weighted and eidx + 1 < NE:
                emit_next_weights(eidx + 1)
        return (t_um, t_dm, staged_next if prefetch_cb is not None else None)

    prev = {0: (None, None), 1: (None, None)}
    lc = {}

    def emit_next_weights(enext):
        nsl = (enext + 1) % 2
        gu, gd = prev[nsl]
        lc[enext] = (load_cast(sb_up[nsl], d_wup[enext], FC, HID, [], guard=gu),
                     load_cast(sb_dn[nsl], d_wdn[enext], HC, D, [], guard=gd))

    tokS = expert_pass(-1, sb_up[0], sb_dn[0], t_supw, t_sdnw, weighted=False,
                       prefetch_cb=lambda: stage_subtile(0, 0, True))
    prev[0] = tokS[:2]
    emit_next_weights(0)
    staged_next = tokS[2]
    for eidx in range(NE):
        sl = (eidx + 1) % 2
        nxt = (lambda e2=eidx + 1: stage_subtile(e2, 0, True))             if eidx + 1 < NE else (lambda: None)
        tok = expert_pass(eidx, sb_up[sl], sb_dn[sl], lc[eidx][0], lc[eidx][1],
                          weighted=True, prefetch_cb=nxt, staged0=staged_next)
        prev[sl] = tok[:2]
        staged_next = tok[2]

    if "g" in d_dbg:
        for st_i in range(NST):
            for mc2 in range(FC):
                p.op("gpsimd", lambda e, mc2=mc2, st_i=st_i: e.dma_start(
                    out=d_dbg["g"][:, mc2, st_i * ST:(st_i + 1) * ST],
                    in_=sb_g[:, mc2 * T + st_i * ST:mc2 * T + st_i * ST + ST]),
                    waits=[t_g_tok[(mc2, st_i)]],)

    tB = [tC.tok(), vC.tok(), sC.tok(), gC.tok()] + p.pool_toks('sync')
    estB2.close()

    if stop_after == "moe":
        p.op("gpsimd", lambda e: e.nop(), waits=p.pool_toks("gpsimd"))
        p.emit()
        return nc

    estC = ExitStack()
    CS = lambda name, shape, dt=F32: estC.enter_context(
        nc.sbuf_tensor(name, shape, dt)).ap()
    sb_xc = [CS(f"xc{i}", [128, FC * ST]) for i in range(2)]
    sb_sc2 = [CS(f"sc2{i}", [128, FC * ST]) for i in range(2)]
    sb_gs = CS("gs", [128, FC * ST])
    sb_oc = [CS(f"oc{i}", [128, FC * ST]) for i in range(2)]

    t_oc_free = [None, None]
    t_cb_prev = None
    for ct in range(NST):
        s0 = ct * ST
        slot = ct % 2
        t_xl = p.dma("sync", lambda e, s0=s0, sl=slot: e.dma_start(
            out=sb_xc[sl].rearrange("pp (f t) -> pp f t", f=FC),
            in_=x3[:, :, s0:s0 + ST]),
            waits=barB + tB + ([t_oc_free[slot]] if t_oc_free[slot] else []))
        t_sl = p.dma("sync", lambda e, s0=s0, sl=slot: e.dma_start(
            out=sb_sc2[sl].rearrange("pp (f t) -> pp f t", f=FC),
            in_=d_s[:, :, s0:s0 + ST]), waits=tB)

        def silu(e, s0=s0):
            for fcm in range(FC):
                ins = e.activation(sb_gs[:, fcm * ST:(fcm + 1) * ST],
                                   sb_g[:, fcm * T + s0:fcm * T + s0 + ST], AF.Silu)
            return ins
        t_si = p.op("scalar", silu,
                    waits=tB + ([t_cb_prev] if t_cb_prev else []), inc=sC)

        def comb1(e, sl=slot):
            for fcm in range(FC):
                ins = e.tensor_mul(sb_oc[sl][:, fcm * ST:(fcm + 1) * ST],
                                   sb_sc2[sl][:, fcm * ST:(fcm + 1) * ST],
                                   sb_gs[:, fcm * ST:(fcm + 1) * ST])
            return ins

        def comb2(e, sl=slot):
            for fcm in range(FC):
                ins = e.tensor_add(sb_oc[sl][:, fcm * ST:(fcm + 1) * ST],
                                   sb_oc[sl][:, fcm * ST:(fcm + 1) * ST],
                                   sb_xc[sl][:, fcm * ST:(fcm + 1) * ST])
            return ins
        t_cb0 = p.op("vector", comb1, waits=[t_xl, t_sl, t_si], inc=vC)
        t_cb = p.op("vector", comb2, waits=[t_cb0], inc=vC)
        t_cb_prev = t_cb
        t_oc_free[slot] = p.dma("gpsimd", lambda e, s0=s0, sl=slot: e.dma_start(
            out=d_out.rearrange("(f pp) t -> pp f t", pp=128)[:, :, s0:s0 + ST],
            in_=sb_oc[sl].rearrange("pp (f t) -> pp f t", f=FC)),
            waits=[t_cb])

    estC.close()
    estB.close()
    p.op("gpsimd", lambda e: e.nop(), waits=p.pool_toks("gpsimd"))
    p.emit()
    return nc


def make_in_map(inputs, core):
    cn = np.ascontiguousarray
    x = np.asarray(inputs["L0_attn_x"])
    m = {
        "x_t": cn(x[core].T),
        "wrT": cn(np.asarray(inputs["L0_attn_W_r"]).T),
        "wkT": cn(np.asarray(inputs["L0_attn_W_k"]).T),
        "wvT": cn(np.asarray(inputs["L0_attn_W_v"]).T),
        "wwT": cn(np.asarray(inputs["L0_attn_W_w"]).T),
        "woT": cn(np.asarray(inputs["L0_attn_W_o"]).T),
        "u_pp": cn(np.asarray(inputs["L0_attn_u"]).reshape(D).reshape(FC, 128).T),
        "tmr_pp": cn(np.asarray(inputs["L0_attn_tm_r"]).reshape(FC, 128).T),
        "tmk_pp": cn(np.asarray(inputs["L0_attn_tm_k"]).reshape(FC, 128).T),
        "tmv_pp": cn(np.asarray(inputs["L0_attn_tm_v"]).reshape(FC, 128).T),
        "tmw_pp": cn(np.asarray(inputs["L0_attn_tm_w"]).reshape(FC, 128).T),
        "lnw_row": cn(np.tile(np.asarray(inputs["L0_attn_ln_w"]), H)),
        "lnb_row": cn(np.tile(np.asarray(inputs["L0_attn_ln_b"]), H)),
        "gate_wT": cn(np.asarray(inputs["L0_ffn_gate_w"]).T),
        "ebias": cn(np.asarray(inputs["L0_ffn_expert_bias"])),
        "w_upT": cn(np.asarray(inputs["L0_ffn_w_up"]).transpose(0, 2, 1)),
        "w_downT": cn(np.asarray(inputs["L0_ffn_w_down"]).transpose(0, 2, 1)),
        "sh_upT": cn(np.asarray(inputs["L0_ffn_shared_w_up"]).T),
        "sh_downT": cn(np.asarray(inputs["L0_ffn_shared_w_down"]).T),
    }
    return {k: np.asarray(v, dtype=np.float32) for k, v in m.items()}


def kernel(**inputs):
    nc = build()
    in_maps = [make_in_map(inputs, c) for c in range(8)]
    res = run_bass_kernel_spmd(nc, in_maps, core_ids=list(range(8)))
    outs = [np.ascontiguousarray(res.results[c]["out_t"].T) for c in range(8)]
    return np.stack(outs, axis=0)
